# version 23
# baseline (speedup 1.0000x reference)
import numpy as np
import concourse.bass as bass
import concourse.mybir as mybir

F32 = mybir.dt.float32
BF16 = mybir.dt.bfloat16
AF = mybir.ActivationFunctionType
ALU = mybir.AluOpType
AX = mybir.AxisListType

SBUF_BYTES = 229376
SBUF_BASE = 16384 + 512


def default_cfg():
    return dict(B=4, S=2048, D=1024, H=16, E=8, O=1024, CH=512, FC=512, OC=512,
                EPS=1e-5, ZEPS=1e-6)


def mini_cfg():
    return dict(B=2, S=1280, D=256, H=4, E=8, O=256, CH=256, FC=256, OC=128,
                EPS=1e-5, ZEPS=1e-6)


class Builder:
    def __init__(self, cfg):
        self.cfg = dict(cfg)
        c = self.cfg
        c["DT"] = c["D"] // 128
        c["NT"] = c["S"] // 128
        c["NCH"] = c["S"] // c["CH"]
        c["TPC"] = c["CH"] // 128
        c["NPAIR"] = c["H"] // 2
        c["NFC"] = (2 * c["D"]) // c["FC"]
        c["NOC"] = c["O"] // c["OC"]
        assert c["H"] * 64 == c["D"]
        self.sems = {}
        self.waited = {}
        self.counts = {}
        self.deps_w = {}
        self.deps_r = {}

    def _merge(self, d, sem, val):
        if d.get(sem, -1) < val:
            d[sem] = val

    def note_w(self, region, sem, val):
        self._merge(self.deps_w.setdefault(region, {}), sem, val)

    def note_r(self, region, sem, val):
        self._merge(self.deps_r.setdefault(region, {}), sem, val)

    def merge_region(self, src, dst, into_r=True):
        for s, v in self.deps_w.get(src, {}).items():
            (self.note_r if into_r else self.note_w)(dst, s, v)
        for s, v in self.deps_r.get(src, {}).items():
            self.note_r(dst, s, v)

    def wait_w(self, eng, engname, region):
        for s, v in self.deps_w.get(region, {}).items():
            self.wait(eng, engname, s, v)

    def wait_rw(self, eng, engname, region):
        self.wait_w(eng, engname, region)
        for s, v in self.deps_r.get(region, {}).items():
            self.wait(eng, engname, s, v)

    def wait(self, eng, engname, semname, val):
        if val <= 0:
            return
        key = (engname, semname)
        if self.waited.get(key, 0) >= val:
            return
        eng.wait_ge(self.sems[semname], val)
        self.waited[key] = val

    def inc(self, inst, semname, k=1):
        inst.then_inc(self.sems[semname], k)
        self.counts[semname] = self.counts.get(semname, 0) + k
        return self.counts[semname]

    def cnt(self, semname):
        return self.counts.get(semname, 0)

    def on(self, engname, fn):
        fn(self._recs[engname])

    class _Inst:
        __slots__ = ("rec",)

        def __init__(self, rec):
            self.rec = rec

        def then_inc(self, sem, k):
            self.rec.append(("__then_inc", (sem, k), {}))
            return self

    class _Rec:

        def __init__(self, outer):
            self.calls = []
            self.outer = outer

        def __getattr__(self, name):
            calls = self.calls

            def method(*a, **kw):
                calls.append((name, a, kw))
                return Builder._Inst(calls)
            return method

        def append(self, x):
            self.calls.append(x)

    def flush_sections(self):
        for engname, rec in self._recs.items():
            calls = rec.calls
            if not calls:
                continue

            def replay(eng, calls=calls):
                last = None
                for name, a, kw in calls:
                    if name == "__then_inc":
                        last.then_inc(*a)
                    else:
                        last = getattr(eng, name)(*a, **kw)
            getattr(self.blk, engname)(replay)

    def build(self):
        c = self.cfg
        nc = bass.Bass("TRN2", target_bir_lowering=False,
                       dynamic_dma_scratch_size=16384)
        self.nc = nc
        B, S, D, H, E, O = c["B"], c["S"], c["D"], c["H"], c["E"], c["O"]
        CH, FC, OC = c["CH"], c["FC"], c["OC"]
        DT, NT, NCH, TPC = c["DT"], c["NT"], c["NCH"], c["TPC"]
        NPAIR = c["NPAIR"]

        dr = {}
        dr["x"] = nc.dram_tensor("x", [B * S, D], F32, kind="ExternalInput").ap()
        dr["wkvT"] = nc.dram_tensor("wkvT", [D, 2 * D], F32, kind="ExternalInput").ap()
        dr["wqT"] = nc.dram_tensor("wqT", [D, D], F32, kind="ExternalInput").ap()
        dr["wpT"] = nc.dram_tensor("wpT", [D, D], F32, kind="ExternalInput").ap()
        dr["bproj"] = nc.dram_tensor("bproj", [1, D], F32, kind="ExternalInput").ap()
        dr["gwT"] = nc.dram_tensor("gwT", [D, E], F32, kind="ExternalInput").ap()
        dr["gb"] = nc.dram_tensor("gb", [1, E], F32, kind="ExternalInput").ap()
        dr["ew"] = nc.dram_tensor("ew", [E, D, O], F32, kind="ExternalInput").ap()
        dr["eb"] = nc.dram_tensor("eb", [E, O], F32, kind="ExternalInput").ap()
        dr["ident"] = nc.dram_tensor("ident", [128, 128], BF16, kind="ExternalInput").ap()
        dr["ones_row"] = nc.dram_tensor("ones_row", [1, 128], BF16, kind="ExternalInput").ap()
        dr["ones_col"] = nc.dram_tensor("ones_col", [128, 1], BF16, kind="ExternalInput").ap()
        dr["sel8"] = nc.dram_tensor("sel8", [E, E * 128], BF16, kind="ExternalInput").ap()
        dr["out"] = nc.dram_tensor("out", [B * S, D], F32, kind="ExternalOutput").ap()
        dr["scr_wkv"] = nc.dram_tensor("scr_wkv", [128, DT * 2 * D], BF16,
                                       kind="Internal").ap()
        dr["scr_wq"] = nc.dram_tensor("scr_wq", [128, DT * D], BF16,
                                      kind="Internal").ap()
        dr["scr_wp"] = nc.dram_tensor("scr_wp", [128, DT * D], BF16,
                                      kind="Internal").ap()
        for e in range(E):
            dr[f"scr_we{e}"] = nc.dram_tensor(
                f"scr_we{e}", [128, DT * O], BF16, kind="Internal").ap()
        if c.get("DBG"):
            dr["dbg"] = nc.dram_tensor("dbg", [128, 16384], F32,
                                       kind="ExternalOutput").ap()
        self.dr = dr

        off = SBUF_BASE

        self._manual_names = set()

        def at(name, shape, dtype, offset):
            assert offset % 32 == 0, (name, offset)
            self._manual_names.add(name)
            return nc.alloc_sbuf_tensor_at(name, shape, dtype, offset=offset)

        def bump(n):
            return (n + 31) & ~31

        a0 = off
        xres = at("xres", [128, NT * D], F32, off); off = bump(off + (NT * D * 4))
        lnx = at("lnx", [128, DT * S], BF16, off); off = bump(off + (DT * S * 2))
        ident = at("ident", [128, 128], BF16, off); off = bump(off + (256))
        ones_row = at("ones_row", [1, 128], BF16, off); off = bump(off + (256))
        ones_col = at("ones_col", [128, 1], BF16, off); off = bump(off + (32))
        bn6 = at("bn6", [128, NT * 12], F32, off); off = bump(off + (NT * 48))
        st2 = at("st2", [128, NT * 2], F32, off); off = bump(off + (NT * 8))
        mean = at("mean", [128, NT], F32, off); off = bump(off + (NT * 4))
        varc = at("varc", [128, NT], F32, off); off = bump(off + (NT * 4))
        rstd = at("rstd", [128, NT], F32, off); off = bump(off + (NT * 4))
        negm = at("negm", [128, NT], F32, off); off = bump(off + (NT * 4))
        mtmp = at("mtmp", [128, NT], F32, off); off = bump(off + (NT * 4))
        scores = at("scores", [128, NT * E], F32, off); off = bump(off + (NT * E * 4))
        bproj_s = at("bproj_s", [1, D], BF16, off); off = bump(off + (D * 2))
        gb_s = at("gb_s", [1, E], BF16, off); off = bump(off + (max(E * 2, 32)))
        gw_s = at("gw_s", [128, DT * E], BF16, off); off = bump(off + (DT * E * 2))
        r_all = at("r_all", [128, H], F32, off); off = bump(off + (H * 4))
        ktw = bump(max(FC, CH) * 2)
        gate_need = 3 * bump(NT * E * 4) + 4 * bump(NT * 4) + bump(NT * E * 2)
        kt0 = off
        ktmp0 = at("ktmp0", [128, max(FC, CH)], BF16, kt0)
        ktmp1 = at("ktmp1", [128, max(FC, CH)], BF16, kt0 + ktw)
        off = bump(kt0 + max(2 * ktw, gate_need))
        g = kt0
        lgraw = at("lgraw", [128, NT * E], F32, g); g = bump(g + NT * E * 4)
        gt1 = at("gt1", [128, NT * E], F32, g); g = bump(g + NT * E * 4)
        gt2 = at("gt2", [128, NT * E], F32, g); g = bump(g + NT * E * 4)
        m1 = at("m1", [128, NT], F32, g); g = bump(g + NT * 4)
        m2 = at("m2", [128, NT], F32, g); g = bump(g + NT * 4)
        ed = at("ed", [128, NT], F32, g); g = bump(g + NT * 4)
        rden = at("rden", [128, NT], F32, g); g = bump(g + NT * 4)
        scores_bf = at("scores_bf", [128, NT * E], BF16, g); g = bump(g + NT * E * 2)
        assert g <= kt0 + max(2 * ktw, gate_need), \
            f"gate tmps overflow ktmp window {g - kt0}"

        c2 = off
        sz_wkv = DT * 2 * D * 2
        sz_wq = DT * D * 2
        sz_wp = DT * D * 2
        sz_kv = TPC * 2 * D * 2
        sz_q = DT * CH * 2
        sz_attB = 2 * D * 2
        sz_kvaug = max(bump(NPAIR * 130 * 2), 4096)
        wkv_s = at("wkv_s", [128, DT * 2 * D], BF16, c2)
        wq_s = at("wq_s", [128, DT * D], BF16, c2 + sz_wkv)
        wp_s = at("wp_s", [128, DT * D], BF16, c2 + sz_wkv + sz_wq)
        kv_s = at("kv_s", [128, TPC * 2 * D], BF16, c2 + sz_wkv + sz_wq + sz_wp)
        q_s = at("q_s", [128, DT * CH], BF16, c2 + sz_wkv + sz_wq + sz_wp + sz_kv)
        attB = at("attB", [128, 2 * D], BF16,
                  c2 + sz_wkv + sz_wq + sz_wp + sz_kv + sz_q)
        kvaug = at("kvaug", [128, NPAIR * 130], BF16,
                   c2 + sz_wkv + sz_wq + sz_wp + sz_kv + sz_q + sz_attB)
        c2_end = (c2 + sz_wkv + sz_wq + sz_wp + sz_kv + sz_q + sz_attB
                  + sz_kvaug)
        ewsz = DT * O * 2
        n_in_c2 = min(E, (c2_end - c2) // ewsz, c.get("NC2", E))
        self.n_in_c2 = n_in_c2
        we = [at(f"we{e}", [128, DT * O], BF16, c2 + e * ewsz)
              for e in range(n_in_c2)]
        attA = at("attA", [128, DT * CH], BF16, c2_end)
        scoresT = at("scoresT", [E, S], BF16, c2_end)
        eb_s = at("eb_s", [E, O], BF16, c2_end + S * 2)
        kvacc = at("kvacc", [128, NPAIR * 65], F32, c2_end)
        tail_p1 = bump(NPAIR * 65 * 4)
        tail_end = c2_end + max(DT * CH * 2, S * 2 + O * 2, tail_p1)
        aoff = a0
        self.affn_spans = []
        for e in range(n_in_c2, E):
            we.append(at(f"we{e}", [128, DT * O], BF16, aoff))
            self.affn_spans.append((f"we{e}", aoff - a0, aoff - a0 + ewsz))
            aoff += ewsz
        EH = E // 2
        lhsT_lo = at("lhsT_lo", [128, EH * DT * 128], BF16, aoff)
        self.affn_spans.append(("lhsT_lo", aoff - a0, aoff - a0 + EH * DT * 256))
        aoff += EH * DT * 128 * 2
        lhsT_hi = at("lhsT_hi", [128, EH * DT * 128], BF16, aoff)
        self.affn_spans.append(("lhsT_hi", aoff - a0, aoff - a0 + EH * DT * 256))
        aoff += EH * DT * 128 * 2
        z8 = at("z8", [128, E * 128], BF16, aoff)
        self.affn_spans.append(("z8", aoff - a0, aoff - a0 + E * 256))
        aoff += E * 128 * 2
        otmp0 = at("otmp0", [128, OC], F32, aoff)
        self.affn_spans.append(("otmp0", aoff - a0, aoff - a0 + OC * 4))
        aoff += OC * 4
        otmp1 = at("otmp1", [128, OC], F32, aoff)
        self.affn_spans.append(("otmp1", aoff - a0, aoff - a0 + OC * 4))
        aoff += OC * 4
        sel8 = at("sel8", [E, E * 128], BF16, aoff)
        self.affn_spans.append(("sel8", aoff - a0, aoff - a0 + E * 256))
        aoff += E * 128 * 2
        assert aoff <= a0 + NT * D * 4, f"ffn-alias overflow {aoff - a0}"
        assert tail_end <= SBUF_BYTES, f"SBUF overflow {tail_end}"
        self.sbuf_used = tail_end

        pb = [nc.alloc_psum_tensor(f"pb{i}", [128, 512], F32) for i in range(6)]
        pbt = [nc.alloc_psum_tensor(f"pbt{i}", [128, 1024], BF16) for i in range(2)]

        semnames = (["s_pe", "s_v", "s_sc", "s_c", "s_wl", "s_st",
                     "s_ac0", "s_ac1", "s_scr", "s_scrk", "s_wls",
                     "s_eb"]
                    + [f"s_we{st}" for st in "ABCD"]
                    + [f"s_ws{st}" for st in "ABCD"]
                    + [f"s_x{ch}" for ch in range(NCH)]
                    + [f"s_wkv{fc}" for fc in range(c["NFC"])]
                    + [f"s_wkvg{fc}" for fc in range(c["NFC"])])
        for name in semnames:
            self.sems[name] = nc.alloc_semaphore(name)

        ns = locals()
        for nm in ["xres", "lnx", "ident", "ones_row", "ones_col", "scores_bf",
                   "bn6", "st2", "mean", "varc", "rstd", "negm", "mtmp",
                   "lgraw", "scores", "gt1", "gt2", "m1", "m2", "ed", "rden",
                   "bproj_s", "gb_s", "gw_s", "kvaug", "r_all",
                   "kvacc",
                   "ktmp0", "ktmp1", "wkv_s", "wq_s", "wp_s", "kv_s", "q_s",
                   "attB", "attA", "scoresT", "eb_s", "we", "lhsT_lo", "sel8",
                   "lhsT_hi", "z8", "otmp0", "otmp1", "pb", "pbt"]:
            setattr(self, nm, ns[nm])

        self.blk = nc.Block()
        self.blk.__enter__()
        self._recs = {e: Builder._Rec(self) for e in
                      ["sync", "gpsimd", "tensor", "vector", "scalar"]}
        self.emit_consts()
        for b in range(B):
            self.emit_batch(b)
        self.emit_tail()
        self.flush_sections()
        self.blk.__exit__(None, None, None)
        self.check_auto_allocs()
        return nc

    def check_auto_allocs(self):
        import concourse.mybir as mb
        for a in self.nc.m.functions[0].allocations:
            if not isinstance(a, mb.MemoryLocationSet):
                continue
            ml = a.memorylocations[0]
            if getattr(ml, "type", None) != "SB":
                continue
            if ml.name.startswith(tuple(self._manual_names)):
                continue
            addr = getattr(ml, "addr", None)
            if addr is None:
                continue
            import concourse.mybir as _mb
            dims = list(ml.dims)
            dt = ml.dtype
            dsz = _mb.dt.size(dt) if dt is not None else 1
            per_part = int(np.prod(dims[1:])) * dsz if len(dims) > 1 else dsz
            end = addr + per_part
            assert end <= SBUF_BASE or addr >= SBUF_BYTES, \
                f"auto alloc {ml.name} at {addr}..{end} collides with manual region"

    def emit_consts(self):
        k = self

        def f(eng):
            k.inc(eng.dma_start(k.ident.ap(), k.dr["ident"]), "s_c", 16)
            k.inc(eng.dma_start(k.ones_row.ap(), k.dr["ones_row"]), "s_c", 16)
            k.inc(eng.dma_start(k.ones_col.ap(), k.dr["ones_col"]), "s_c", 16)
            tot = k.cnt("s_c")
            for r in ["ident", "ones_row", "ones_col"]:
                k.note_w(r, "s_c", tot)
        self.on("sync", f)

    def emit_weight_loads(self, b):
        k = self
        c = self.cfg
        DT, D, FC = c["DT"], c["D"], c["FC"]
        NFC = c["NFC"]

        def f(eng):
            regs = ["wkv_s", "wq_s", "wp_s", "gw_s", "bproj_s", "gb_s"]
            for r in regs:
                k.wait_rw(eng, "gpsimd" if b == 0 else "sync", r)
            for r in regs:
                self.deps_w[r] = {}
                self.deps_r.pop(r, None)
            if b == 0:
                for fc in range(NFC):
                    vw = k.wkv_s.ap().rearrange(
                        "p (t n) -> p t n", t=DT)[:, :, fc * FC:(fc + 1) * FC]
                    vs = k.dr["wkvT"].rearrange(
                        "(t p) n -> p t n", p=128)[:, :, fc * FC:(fc + 1) * FC]
                    i = k.inc(eng.dma_start(vw, vs), f"s_wkvg{fc}", 16)
                    k.note_w(f"wkv_fc{fc}", f"s_wkvg{fc}", i)
                    k.note_w("wkv_s", f"s_wkvg{fc}", i)
                for r, dsrc, dst, nt in [("wq_s", "wqT", k.wq_s, DT),
                                         ("wp_s", "wpT", k.wp_s, DT)]:
                    k.inc(eng.dma_start(
                        dst.ap().rearrange("p (t n) -> p t n", t=nt),
                        k.dr[dsrc].rearrange("(t p) n -> p t n", p=128)),
                        "s_wl", 16)
        if b == 0:
            self.on("gpsimd", f)
        else:
            def fqp(eng):
                for r in ["wq_s", "wp_s"]:
                    k.wait_rw(eng, "scalar", r)
                for r in ["wq_s", "wp_s"]:
                    self.deps_w[r] = {}
                    self.deps_r.pop(r, None)
                k.wait(eng, "scalar", "s_scr", self._scr_total)
                k.inc(eng.dma_start(k.wq_s.ap(), k.dr["scr_wq"]), "s_wls", 16)
                i = k.inc(eng.dma_start(k.wp_s.ap(), k.dr["scr_wp"]), "s_wls", 16)
                k.note_w("wq_s", "s_wls", i)
                k.note_w("wp_s", "s_wls", i)
            self.on("scalar", fqp)

        def f2(eng):
            for r in ["gw_s", "bproj_s", "gb_s"]:
                k.wait_rw(eng, "gpsimd", r)
            for r, dsrc, dst, nt in [("gw_s", "gwT", k.gw_s, DT)]:
                k.inc(eng.dma_start(
                    dst.ap().rearrange("p (t n) -> p t n", t=nt),
                    k.dr[dsrc].rearrange("(t p) n -> p t n", p=128)), "s_wl", 16)
            for r, dsrc, dst in [("bproj_s", "bproj", k.bproj_s),
                                 ("gb_s", "gb", k.gb_s)]:
                k.inc(eng.dma_start(dst.ap(), k.dr[dsrc]), "s_wl", 16)
            tot = k.cnt("s_wl")
            for r in ["gw_s", "bproj_s", "gb_s", "wq_s", "wp_s"]:
                k.note_w(r, "s_wl", tot)
        self.on("gpsimd", f2)

    def emit_wkv_reload(self, fc):
        k = self
        c = self.cfg
        DT, FC = c["DT"], c["FC"]

        def f(eng, fc=fc):
            if fc == 0:
                k.wait_rw(eng, "sync", "wkv_s")
                self.deps_w["wkv_s"] = {}
                self.deps_r.pop("wkv_s", None)
            k.wait(eng, "sync", "s_scrk", 16)
            vw = k.wkv_s.ap().rearrange(
                "p (t n) -> p t n", t=DT)[:, :, fc * FC:(fc + 1) * FC]
            vs = k.dr["scr_wkv"].rearrange(
                "p (t n) -> p t n", t=DT)[:, :, fc * FC:(fc + 1) * FC]
            i = k.inc(eng.dma_start(vw, vs), f"s_wkv{fc}", 16)
            k.note_w(f"wkv_fc{fc}", f"s_wkv{fc}", i)
            k.note_w("wkv_s", f"s_wkv{fc}", i)
        self.on("sync", f)

    def emit_expert_load(self, e, b, stage="D"):
        k = self
        DT = self.cfg["DT"]
        self.bind_we_span(e)
        self._we_loaded.add(e)
        if b == 0:
            sem = f"s_we{stage}"

            def f(eng):
                k.wait_rw(eng, "gpsimd", f"we{e}")
                self.deps_w[f"we{e}"] = {}
                self.deps_r.pop(f"we{e}", None)
                k.wait(eng, "gpsimd", "s_scr", self._scr_total)
                i = k.inc(eng.dma_start(
                    k.we[e].ap().rearrange("p (t n) -> p t n", t=DT),
                    k.dr["ew"][e].rearrange("(t p) n -> p t n", p=128)),
                    sem, 16)
                self._we_pend.append((e, sem))
            self.on("gpsimd", f)
        else:
            sem = f"s_ws{stage}"

            def f(eng):
                k.wait_rw(eng, "sync", f"we{e}")
                self.deps_w[f"we{e}"] = {}
                self.deps_r.pop(f"we{e}", None)
                k.wait(eng, "sync", "s_scr", self._scr_total)
                i = k.inc(eng.dma_start(
                    k.we[e].ap(), k.dr[f"scr_we{e}"]), sem, 16)
                self._we_pend.append((e, sem))
            self.on("sync", f)

    def flush_we_notes(self):
        for e, sem in self._we_pend:
            self.note_w(f"we{e}", sem, self.cnt(sem))
        self._we_pend = []

    def emit_scratch_w_qkv(self):
        k = self

        def f(eng):
            for fc in range(self.cfg["NFC"]):
                k.wait_w(eng, "scalar", f"wkv_fc{fc}")
            k.wait_w(eng, "scalar", "wq_s")
            k.wait_w(eng, "scalar", "wp_s")
            i = k.inc(eng.dma_start(k.dr["scr_wkv"], k.wkv_s.ap()),
                      "s_scrk", 16)
            k.note_r("wkv_s", "s_scrk", i)
            k.inc(eng.dma_start(k.dr["scr_wq"], k.wq_s.ap()), "s_scr", 16)
            k.inc(eng.dma_start(k.dr["scr_wp"], k.wp_s.ap()), "s_scr", 16)
            t = k.cnt("s_scr")
            for r in ["wq_s", "wp_s"]:
                k.note_r(r, "s_scr", t)
        self.on("scalar", f)

    def emit_scratch_we(self):
        k = self

        def f(eng):
            for st in "ABCD":
                k.wait(eng, "scalar", f"s_we{st}", k.cnt(f"s_we{st}"))
            for e in range(self.cfg["E"]):
                k.inc(eng.dma_start(
                    k.dr[f"scr_we{e}"], k.we[e].ap()), "s_scr", 16)
            self._scr_total = k.cnt("s_scr")
            for e in range(self.cfg["E"]):
                k.note_r(f"we{e}", "s_scr", self._scr_total)
        self.on("scalar", f)

    def _suite_bounds(self):
        c = self.cfg
        suite = [("wkv_s", c["DT"] * 2 * c["D"] * 2),
                 ("wq_s", c["DT"] * c["D"] * 2),
                 ("wp_s", c["DT"] * c["D"] * 2),
                 ("kv_s", c["TPC"] * 2 * c["D"] * 2),
                 ("q_s", c["DT"] * c["CH"] * 2),
                 ("attB", 2 * c["D"] * 2),
                 ("kvaug", (c["NPAIR"] * 130 * 2 + 31) & ~31)]
        pos = 0
        bounds = []
        for nm, sz in suite:
            bounds.append((nm, pos, pos + sz))
            pos += sz
        return bounds

    def bind_we_span(self, e):
        c = self.cfg
        ewsz = c["DT"] * c["O"] * 2
        if e < self.n_in_c2:
            lo, hi = e * ewsz, (e + 1) * ewsz
            for nm, rlo, rhi in self._suite_bounds():
                if rlo < hi and lo < rhi:
                    self.merge_region(nm, f"we{e}")
        else:
            lo, hi = next((l, h) for nm, l, h in self.affn_spans
                          if nm == f"we{e}")
            for t in range(c["NT"]):
                tlo, thi = t * c["D"] * 4, (t + 1) * c["D"] * 4
                if tlo < hi and lo < thi:
                    self.merge_region(f"x{t}", f"we{e}")

    def bind_we_aliases(self):
        c = self.cfg
        for e in range(self.cfg["E"]):
            self.bind_we_span(e)
        for nm, lo, hi in self.affn_spans:
            if nm.startswith("we"):
                continue
            for t in range(c["NT"]):
                tlo, thi = t * c["D"] * 4, (t + 1) * c["D"] * 4
                if tlo < hi and lo < thi:
                    self.merge_region(f"x{t}", nm)

    def emit_ln_stats(self):
        for ch in range(self.cfg["NCH"]):
            self.emit_ln_stats_chunk(ch)
            self.emit_ln_chain(ch)

    def emit_ln_stats_chunk(self, ch):
        k = self
        c = self.cfg
        D, NT, NCH, TPC = c["D"], c["NT"], c["NCH"], c["TPC"]
        if True:
            tiles = [ch * TPC + m for m in range(TPC)]

            def f_v(eng, tiles=tiles, ch=ch):
                for t in tiles:
                    k.wait_w(eng, "vector", f"x{t}")
                    k.wait_rw(eng, "vector", f"s1c{ch}")
                    i = k.inc(eng.tensor_reduce(
                        k.st2.ap()[:, t * 2:t * 2 + 1],
                        k.xres.ap()[:, t * D:(t + 1) * D],
                        axis=AX.X, op=ALU.add), "s_v")
                    k.note_r(f"x{t}", "s_v", i)
                k.note_w(f"s1c{ch}", "s_v", i)
            self.on("vector", f_v)

            def f_s(eng, tiles=tiles, ch=ch):
                ns = max(1, (c["DT"] * c["CH"]) // D)
                for t in tiles:
                    m = t % ns
                    k.wait_w(eng, "scalar", f"x{t}")
                    k.wait_rw(eng, "scalar", "q_s")
                    k.wait_rw(eng, "scalar", f"s2c{ch}")
                    i = k.inc(eng.activation(
                        k.q_s.ap()[:, m * D:(m + 1) * D],
                        k.xres.ap()[:, t * D:(t + 1) * D],
                        AF.Square, accum_out=k.st2.ap()[:, t * 2 + 1:t * 2 + 2]),
                        "s_sc")
                    k.note_r(f"x{t}", "s_sc", i)
                    k.note_w("q_s", "s_sc", i)
                    k.note_w(f"q_sl{m}", "s_sc", i)
                k.note_w(f"s2c{ch}", "s_sc", i)
            self.on("scalar", f_s)

    def emit_ln_chain(self, ch):
        k = self
        c = self.cfg
        D, NT, NCH, TPC = c["D"], c["NT"], c["NCH"], c["TPC"]
        inv_d = 1.0 / D
        sl = slice(ch * TPC, (ch + 1) * TPC)
        ln = f"lnstat{ch}"

        def f_v2(eng):
            k.wait_w(eng, "vector", f"s1c{ch}")
            k.wait_w(eng, "vector", f"s2c{ch}")
            k.wait_rw(eng, "vector", ln)
            sv = k.st2.ap()[:, ch * TPC * 2:(ch + 1) * TPC * 2].rearrange(
                "p (t x) -> p t x", x=2)
            k.inc(eng.tensor_scalar(
                k.mean.ap()[:, sl], sv[:, :, 0:1].squeeze(2),
                scalar1=inv_d, scalar2=None, op0=ALU.mult), "s_v")
            k.wait(eng, "vector", "s_v", k.cnt("s_v"))
            k.inc(eng.tensor_tensor(
                k.mtmp.ap()[:, sl], k.mean.ap()[:, sl],
                k.mean.ap()[:, sl], op=ALU.mult), "s_v")
            k.wait(eng, "vector", "s_v", k.cnt("s_v"))
            k.inc(eng.scalar_tensor_tensor(
                k.varc.ap()[:, sl], sv[:, :, 1:2].squeeze(2), inv_d,
                k.mtmp.ap()[:, sl], op0=ALU.mult, op1=ALU.subtract), "s_v")
            k.wait(eng, "vector", "s_v", k.cnt("s_v"))
            i = k.inc(eng.tensor_scalar(
                k.varc.ap()[:, sl], k.varc.ap()[:, sl],
                scalar1=float(c["EPS"]), scalar2=None, op0=ALU.add), "s_v")
            k.note_w(ln, "s_v", i)
            k.note_r(f"s1c{ch}", "s_v", i)
            k.note_r(f"s2c{ch}", "s_v", i)
        self.on("vector", f_v2)

        def f_s2(eng):
            k.wait_w(eng, "scalar", ln)
            k.inc(eng.activation(
                k.rstd.ap()[:, sl], k.varc.ap()[:, sl], AF.Ln), "s_sc")
            k.wait(eng, "scalar", "s_sc", k.cnt("s_sc"))
            i = k.inc(eng.activation(
                k.rstd.ap()[:, sl], k.rstd.ap()[:, sl],
                AF.Exp, scale=-0.5), "s_sc")
            k.note_w(ln, "s_sc", i)
        self.on("scalar", f_s2)

        def f_v3(eng):
            k.wait_w(eng, "vector", ln)
            i = k.inc(eng.scalar_tensor_tensor(
                k.negm.ap()[:, sl], k.mean.ap()[:, sl], -1.0,
                k.rstd.ap()[:, sl], op0=ALU.mult, op1=ALU.mult), "s_v")
            k.note_w(ln, "s_v", i)
        self.on("vector", f_v3)

    def _ln_buf(self, dst):
        c = self.cfg
        if dst == "q":
            return (self.q_s, max(1, (c["DT"] * c["CH"]) // c["D"]),
                    "q_s", "q_sl")
        return (self.kv_s, c["TPC"] * 2, "kv_s", "kv_sl")

    def emit_ln_n(self, ch, dst="q"):
        k = self
        c = self.cfg
        D, TPC = c["D"], c["TPC"]
        buf, ns, coarse, slp = self._ln_buf(dst)
        for t in [ch * TPC + m for m in range(TPC)]:
            m = t % ns

            def f_n(eng, t=t, m=m):
                k.wait_w(eng, "vector", f"lnstat{t // self.cfg['TPC']}")
                k.wait_w(eng, "vector", f"x{t}")
                k.wait_rw(eng, "vector", coarse)
                j = k.inc(eng.scalar_tensor_tensor(
                    buf.ap()[:, m * D:(m + 1) * D],
                    k.xres.ap()[:, t * D:(t + 1) * D],
                    k.rstd.ap()[:, t:t + 1],
                    k.negm.ap()[:, t:t + 1].broadcast_to([128, D]),
                    op0=ALU.mult, op1=ALU.add), "s_v")
                k.note_r(f"x{t}", "s_v", j)
                k.note_w(coarse, "s_v", j)
                k.note_w(f"{slp}{m}", "s_v", j)
            self.on("vector", f_n)

    def emit_ln_t(self, ch, ev="vector", src="q", defer=False):
        k = self
        c = self.cfg
        D, DT, TPC = c["D"], c["DT"], c["TPC"]
        buf, ns, coarse, slp = self._ln_buf(src)
        sem = "s_v" if ev == "vector" else "s_sc"
        out = []
        for t in [ch * TPC + m for m in range(TPC)]:
            m = t % ns

            def f_pe(eng, t=t, m=m):
                bank = 6 + (t % 2)
                k.wait_w(eng, "tensor", "ident")
                k.wait_w(eng, "tensor", f"{slp}{m}")
                k.wait_rw(eng, "tensor", f"bank{bank}")
                for dt in range(DT):
                    inst = eng.transpose(
                        k.pbt[bank - 6].ap()[:, dt * 128:(dt + 1) * 128],
                        buf.ap()[:, m * D + dt * 128: m * D + (dt + 1) * 128],
                        k.ident.ap())
                i = k.inc(inst, "s_pe")
                k.note_w(f"bank{bank}", "s_pe", i)
                k.note_r(coarse, "s_pe", i)

            def f_v4(eng, t=t, ch=ch):
                bank = 6 + (t % 2)
                k.wait_w(eng, ev, f"bank{bank}")
                k.wait_rw(eng, ev, f"lnx{ch}")
                dst = k.lnx.ap().rearrange("p (dt s) -> p dt s", dt=DT)[
                    :, :, t * 128:(t + 1) * 128]
                srcv = k.pbt[bank - 6].ap()[:, 0:DT * 128].rearrange(
                    "p (dt n) -> p dt n", n=128)
                if ev == "vector":
                    i = k.inc(eng.tensor_copy(dst, srcv), sem)
                else:
                    i = k.inc(eng.copy(dst, srcv), sem)
                k.note_r(f"bank{bank}", sem, i)
                k.note_w(f"lnx{ch}", sem, i)
                k.note_w(f"lnxT{t}", sem, i)
            if defer:
                out.append([("tensor", f_pe), (ev, f_v4)])
            else:
                self.on("tensor", f_pe)
                self.on(ev, f_v4)
        return out

    def emit_batch(self, b):
        k = self
        c = self.cfg
        B, S, D, H, E, O = c["B"], c["S"], c["D"], c["H"], c["E"], c["O"]
        CH, FC, OC = c["CH"], c["FC"], c["OC"]
        DT, NT, NCH, TPC = c["DT"], c["NT"], c["NCH"], c["TPC"]
        NFC, NOC, NPAIR = c["NFC"], c["NOC"], c["NPAIR"]
        self._we_loaded = set()
        self._we_pend = []
        EH = E // 2
        nhalf = max(D // 512, 1)
        hw = min(512, D)
        NG = max(H // 4, 1)
        HPG = min(4, H)

        for dst in ["kvacc"]:
            self.merge_region("attA", dst)
            self.merge_region("scoresT", dst)
            self.merge_region("eb_s", dst)

        for ch in range(NCH):
            def f_x(eng, ch=ch):
                tiles = [ch * TPC + m for m in range(TPC)]
                for t in tiles:
                    k.wait_rw(eng, "sync", f"x{t}")
                k.wait_rw(eng, "sync", f"ffnA{ch}")
                for t in tiles:
                    self.deps_w[f"x{t}"] = {}
                    self.deps_r.pop(f"x{t}", None)
                row0 = b * S + ch * CH
                dst = k.xres.ap()[:, ch * TPC * D:(ch + 1) * TPC * D].rearrange(
                    "p (t d) -> p t d", t=TPC)
                src = k.dr["x"][row0:row0 + CH, :].rearrange(
                    "(t p) d -> p t d", p=128)
                i = k.inc(eng.dma_start(dst, src), f"s_x{ch}", 16)
                for t in tiles:
                    k.note_w(f"x{t}", f"s_x{ch}", i)
            self.on("sync", f_x)
            if b > 0 and ch < NFC:
                self.emit_wkv_reload(ch)
        if b > 0:
            for fc in range(min(NCH, NFC), NFC):
                self.emit_wkv_reload(fc)
        if b == 0:
            self._scr_total = 0
            self.emit_weight_loads(0)

        def cut_dump():
            def f_cd(eng):
                for t in range(NT):
                    k.wait_w(eng, "sync", f"x{t}")
                    row0 = b * S + t * 128
                    k.inc(eng.dma_start(
                        k.dr["out"][row0:row0 + 128, :],
                        k.xres.ap()[:, t * D:(t + 1) * D]), "s_st", 16)
                tot = k.cnt("s_st")
                for t in range(NT):
                    k.note_r(f"x{t}", "s_st", tot)
            self.on("sync", f_cd)

        cut = self.cfg.get("CUT")
        if cut == "load":
            cut_dump(); return

        self.emit_ln_stats_chunk(0)
        self.emit_ln_chain(0)
        if cut == "ln1":
            for ch in range(NCH):
                if ch > 0:
                    self.emit_ln_stats_chunk(ch)
                    self.emit_ln_chain(ch)
                self.emit_ln_n(ch)
                self.emit_ln_t(ch)
            cut_dump(); return

        for ch in range(NCH):
            if ch == 0:
                self.emit_ln_n(0)
            self.emit_ln_t(ch)
            if ch + 1 < NCH:
                self.emit_ln_stats_chunk(ch + 1)
                self.emit_ln_chain(ch + 1)
                self.emit_ln_n(ch + 1)
            for m in range(TPC):
                t = ch * TPC + m
                for fc in range(NFC):
                    bank = (m * NFC + fc) % 4
                    is_k = (fc * FC) < D

                    def f_kv(eng, t=t, m=m, fc=fc, bank=bank, ch=ch):
                        k.wait_w(eng, "tensor", f"lnxT{t}")
                        k.wait_w(eng, "tensor", f"wkv_fc{fc}")
                        k.wait_rw(eng, "tensor", f"bank{bank}")
                        for dt in range(DT):
                            inst = eng.matmul(
                                k.pb[bank].ap()[:, 0:FC],
                                k.lnx.ap()[:, dt * S + t * 128: dt * S + (t + 1) * 128],
                                k.wkv_s.ap()[:, dt * 2 * D + fc * FC:
                                             dt * 2 * D + (fc + 1) * FC],
                                start=(dt == 0), stop=(dt == DT - 1))
                        i = k.inc(inst, "s_pe")
                        k.note_w(f"bank{bank}", "s_pe", i)
                        k.note_r(f"lnx{ch}", "s_pe", i)
                        k.note_r("wkv_s", "s_pe", i)
                    self.on("tensor", f_kv)

                    if is_k:
                        def f_sq(eng, m=m, fc=fc, bank=bank):
                            kt = k.ktmp0 if bank % 2 == 0 else k.ktmp1
                            k.wait_w(eng, "scalar", f"bank{bank}")
                            k.wait_rw(eng, "scalar", f"ktmp{bank % 2}")
                            i = k.inc(eng.activation(
                                kt.ap()[:, 0:FC], k.pb[bank].ap()[:, 0:FC],
                                AF.Square), "s_sc")
                            k.note_w(f"ktmp{bank % 2}", "s_sc", i)
                            k.note_r(f"bank{bank}", "s_sc", i)
                        self.on("scalar", f_sq)

                        def f_ex(eng, m=m, fc=fc, bank=bank):
                            kt = k.ktmp0 if bank % 2 == 0 else k.ktmp1
                            k.wait_w(eng, "scalar", f"ktmp{bank % 2}")
                            k.wait_rw(eng, "scalar", "kv_s")
                            i = k.inc(eng.activation(
                                k.kv_s.ap()[:, m * 2 * D + fc * FC:
                                            m * 2 * D + (fc + 1) * FC],
                                kt.ap()[:, 0:FC], AF.Exp, scale=-0.5), "s_sc")
                            k.note_r(f"ktmp{bank % 2}", "s_sc", i)
                            k.note_w("kv_s", "s_sc", i)
                        self.on("scalar", f_ex)
                    else:
                        def f_cp(eng, m=m, fc=fc, bank=bank):
                            k.wait_w(eng, "scalar", f"bank{bank}")
                            k.wait_rw(eng, "scalar", "kv_s")
                            i = k.inc(eng.copy(
                                k.kv_s.ap()[:, m * 2 * D + fc * FC:
                                            m * 2 * D + (fc + 1) * FC],
                                k.pb[bank].ap()[:, 0:FC]), "s_sc")
                            k.note_r(f"bank{bank}", "s_sc", i)
                            k.note_w("kv_s", "s_sc", i)
                        self.on("scalar", f_cp)

            def f_kvaug(eng, ch=ch):
                k.wait_w(eng, "tensor", "kv_s")
                k.wait_w(eng, "tensor", "ones_col")
                for bk in ([4, 5] if NPAIR > 4 else [4]) + \
                        [2 + x for x in range(nhalf)]:
                    k.wait_rw(eng, "tensor", f"bank{bk}")
                for h in range(H):
                    p = h // 2
                    bank = 4 + (p // 4)
                    prow = (h % 2) * 64
                    pcol = (p % 4) * 65
                    for m in range(TPC):
                        inst = eng.matmul(
                            k.pb[bank].ap()[prow:prow + 64, pcol:pcol + 64],
                            k.kv_s.ap()[:, m * 2 * D + h * 64:
                                        m * 2 * D + h * 64 + 64],
                            k.kv_s.ap()[:, m * 2 * D + D + h * 64:
                                        m * 2 * D + D + h * 64 + 64],
                            start=(m == 0), stop=(m == TPC - 1))
                    for m in range(TPC):
                        inst = eng.matmul(
                            k.pb[bank].ap()[prow:prow + 64, pcol + 64:pcol + 65],
                            k.kv_s.ap()[:, m * 2 * D + h * 64:
                                        m * 2 * D + h * 64 + 64],
                            k.ones_col.ap(),
                            start=(m == 0), stop=(m == TPC - 1))
                i = k.inc(inst, "s_pe")
                for bk in ([4, 5] if NPAIR > 4 else [4]):
                    k.note_w(f"bank{bk}", "s_pe", i)
                k.note_r("kv_s", "s_pe", i)
                k.note_r("ones_col", "s_pe", i)
            self.on("tensor", f_kvaug)

            def f_kvacc(eng, ch=ch):
                op = ALU.add
                np_lo = min(NPAIR, 4)
                k.wait_w(eng, "vector", "bank4")
                k.wait_rw(eng, "vector", "kvacc")
                src = k.pb[4].ap()[:, 0:np_lo * 65]
                dstv = k.kvacc.ap()[:, 0:np_lo * 65]
                if ch == 0:
                    i = k.inc(eng.tensor_copy(dstv, src), "s_v")
                else:
                    i = k.inc(eng.tensor_tensor(dstv, dstv, src, op=op), "s_v")
                k.note_r("bank4", "s_v", i)
                if NPAIR > 4:
                    k.wait_w(eng, "vector", "bank5")
                    src = k.pb[5].ap()[:, 0:(NPAIR - 4) * 65]
                    dstv = k.kvacc.ap()[:, 4 * 65:NPAIR * 65]
                    if ch == 0:
                        i = k.inc(eng.tensor_copy(dstv, src), "s_v")
                    else:
                        i = k.inc(eng.tensor_tensor(dstv, dstv, src, op=op), "s_v")
                    k.note_r("bank5", "s_v", i)
                k.note_w("kvacc", "s_v", i)
            self.on("vector", f_kvacc)

        def f_kvev(eng):
            k.wait_rw(eng, "vector", "kvaug")
            k.wait_w(eng, "vector", "kvacc")
            kva = k.kvaug.ap().rearrange("p (pr x) -> p pr x", pr=NPAIR)
            k.inc(eng.memset(kva[0:64, :, 64:130], 0.0), "s_v")
            k.inc(eng.memset(kva[64:128, :, 0:64], 0.0), "s_v")
            k.inc(eng.memset(kva[64:128, :, 128:129], 0.0), "s_v")
            k.wait(eng, "vector", "s_v", k.cnt("s_v"))
            srcv = k.kvacc.ap().rearrange("p (pr x) -> p pr x", x=65)
            k.inc(eng.tensor_copy(
                kva[0:64, :, 0:64], srcv[0:64, :, 0:64]), "s_v")
            k.inc(eng.tensor_copy(
                kva[64:128, :, 64:128], srcv[64:128, :, 0:64]), "s_v")
            k.inc(eng.tensor_copy(
                kva[0:64, :, 128:129], srcv[0:64, :, 64:65]), "s_v")
            i = k.inc(eng.tensor_copy(
                kva[64:128, :, 129:130], srcv[64:128, :, 64:65]), "s_v")
            k.note_r("kvacc", "s_v", i)
            k.note_w("kvaug", "s_v", i)
        self.on("vector", f_kvev)

        for srcr in ["kvacc"]:
            self.merge_region(srcr, "attA")
        if cut == "p1":
            cut_dump(); return

        if b == 0:
            self.emit_scratch_w_qkv()
        ewsz = DT * O * 2
        for e in range(self.n_in_c2):
            if (e + 1) * ewsz <= DT * 2 * D * 2:
                self.emit_expert_load(e, b, stage="A")

        p2sub = self.cfg.get("P2SUB")
        self._p2_pend = []
        self._ln2_pend = []
        for ch in range(NCH):
            for f in range(DT):
                bank = f % 4

                def f_q(eng, f=f, bank=bank, ch=ch):
                    k.wait_w(eng, "tensor", f"lnx{ch}")
                    k.wait_w(eng, "tensor", "wq_s")
                    k.wait_rw(eng, "tensor", f"bank{bank}")
                    for dt in range(DT):
                        inst = eng.matmul(
                            k.pb[bank].ap()[:, 0:CH],
                            k.wq_s.ap()[:, dt * D + f * 128: dt * D + (f + 1) * 128],
                            k.lnx.ap()[:, dt * S + ch * CH: dt * S + (ch + 1) * CH],
                            start=(dt == 0), stop=(dt == DT - 1))
                    i = k.inc(inst, "s_pe")
                    k.note_w(f"bank{bank}", "s_pe", i)
                    k.note_r(f"lnx{ch}", "s_pe", i)
                    k.note_r("wq_s", "s_pe", i)
                self.on("tensor", f_q)

                def f_qe(eng, f=f, bank=bank):
                    kt = k.ktmp0 if bank % 2 == 0 else k.ktmp1
                    k.wait_w(eng, "scalar", f"bank{bank}")
                    k.wait_rw(eng, "scalar", f"ktmp{bank % 2}")
                    i = k.inc(eng.activation(
                        kt.ap()[:, 0:CH], k.pb[bank].ap()[:, 0:CH],
                        AF.Square), "s_sc")
                    k.note_w(f"ktmp{bank % 2}", "s_sc", i)
                    k.note_r(f"bank{bank}", "s_sc", i)
                self.on("scalar", f_qe)

                def f_qx(eng, f=f, bank=bank):
                    kt = k.ktmp0 if bank % 2 == 0 else k.ktmp1
                    k.wait_w(eng, "scalar", f"ktmp{bank % 2}")
                    k.wait_rw(eng, "scalar", "q_s")
                    i = k.inc(eng.activation(
                        k.q_s.ap()[:, f * CH:(f + 1) * CH],
                        kt.ap()[:, 0:CH], AF.Exp, scale=-0.5), "s_sc")
                    k.note_r(f"ktmp{bank % 2}", "s_sc", i)
                    k.note_w("q_s", "s_sc", i)
                self.on("scalar", f_qx)

            if p2sub == "q":
                continue
            for m in range(TPC):
                t = ch * TPC + m
                abank0 = 2 + (m % 2) * 2
                dbank = m % 2

                def f_att(eng, m=m, ch=ch, abank0=abank0, dbank=dbank):
                    k.wait_w(eng, "tensor", "q_s")
                    k.wait_w(eng, "tensor", "kvaug")
                    k.wait_rw(eng, "tensor", f"bank{abank0}")
                    k.wait_rw(eng, "tensor", f"bank{abank0 + 1}")
                    k.wait_rw(eng, "tensor", f"bank{dbank}")
                    for p in range(NPAIR):
                        bank = abank0 + p // 4
                        eng.matmul(
                            k.pb[bank].ap()[:, (p % 4) * 128:(p % 4 + 1) * 128],
                            k.q_s.ap()[:, p * CH + m * 128:
                                       p * CH + (m + 1) * 128],
                            k.kvaug.ap()[:, p * 130:p * 130 + 128],
                            start=True, stop=True)
                    for p in range(NPAIR):
                        inst = eng.matmul(
                            k.pb[dbank].ap()[:, p * 2:(p + 1) * 2],
                            k.q_s.ap()[:, p * CH + m * 128:
                                       p * CH + (m + 1) * 128],
                            k.kvaug.ap()[:, p * 130 + 128:p * 130 + 130],
                            start=True, stop=True)
                    i = k.inc(inst, "s_pe")
                    k.note_w(f"bank{abank0}", "s_pe", i)
                    k.note_w(f"bank{abank0 + 1}", "s_pe", i)
                    k.note_w(f"bank{dbank}", "s_pe", i)
                    k.note_r("q_s", "s_pe", i)
                    k.note_r("kvaug", "s_pe", i)
                self.on("tensor", f_att)

                while len(self._p2_pend) > 1:
                    for engn, fn in self._p2_pend.pop(0):
                        self.on(engn, fn)
                if self._ln2_pend:
                    for engn, fn in self._ln2_pend.pop(0):
                        self.on(engn, fn)

                def f_z(eng, m=m, ch=ch, abank0=abank0, dbank=dbank):
                    k.wait_w(eng, "vector", f"bank{dbank}")
                    k.wait_rw(eng, "vector", "r_all")
                    k.inc(eng.tensor_scalar(
                        k.r_all.ap(), k.pb[dbank].ap()[:, 0:H],
                        scalar1=float(c["ZEPS"]), scalar2=None,
                        op0=ALU.add), "s_v")
                    k.wait(eng, "vector", "s_v", k.cnt("s_v"))
                    i = k.inc(eng.reciprocal(k.r_all.ap(), k.r_all.ap()), "s_v")
                    k.note_w("r_all", "s_v", i)
                    k.note_r(f"bank{dbank}", "s_v", i)
                    k.wait(eng, "vector", "s_v", i)
                    k.wait_rw(eng, "vector", "attB")
                    for j in range(2 if NPAIR > 4 else 1):
                        bank = abank0 + j
                        npp = min(4, NPAIR - j * 4)
                        k.wait_w(eng, "vector", f"bank{bank}")
                        r = k.r_all.ap()[:, j * 8:j * 8 + npp * 2].unsqueeze(2)
                        i = k.inc(eng.tensor_tensor(
                            k.attB.ap()[:, (m % 2) * D + j * 512:
                                        (m % 2) * D + j * 512 + npp * 128
                                        ].rearrange("p (hh x) -> p hh x", x=64),
                            k.pb[bank].ap()[:, 0:npp * 128].rearrange(
                                "p (hh x) -> p hh x", x=64),
                            r.broadcast_to([128, npp * 2, 64]), op=ALU.mult),
                            "s_v")
                        k.note_r(f"bank{bank}", "s_v", i)
                    k.note_w("attB", "s_v", i)
                    k.note_r("r_all", "s_v", i)
                self.on("vector", f_z)

                ttpend = []

                def f_tt(eng, m=m, ch=ch):
                    bank = 6 + (m % 2)
                    k.wait_w(eng, "tensor", "attB")
                    k.wait_rw(eng, "tensor", f"bank{bank}")
                    for dt in range(DT):
                        inst = eng.transpose(
                            k.pbt[bank - 6].ap()[:, dt * 128:(dt + 1) * 128],
                            k.attB.ap()[:, (m % 2) * D + dt * 128:
                                        (m % 2) * D + (dt + 1) * 128],
                            k.ident.ap())
                    i = k.inc(inst, "s_pe")
                    k.note_w(f"bank{bank}", "s_pe", i)
                    k.note_r("attB", "s_pe", i)
                ttpend.append(("tensor", f_tt))

                def f_te(eng, m=m):
                    bank = 6 + (m % 2)
                    k.wait_w(eng, "scalar", f"bank{bank}")
                    k.wait_rw(eng, "scalar", "attA")
                    dst = k.attA.ap().rearrange("p (ct n) -> p ct n", ct=DT)[
                        :, :, m * 128:(m + 1) * 128]
                    src = k.pbt[bank - 6].ap()[:, 0:DT * 128].rearrange(
                        "p (ct n) -> p ct n", n=128)
                    i = k.inc(eng.copy(dst, src), "s_sc")
                    k.note_r(f"bank{bank}", "s_sc", i)
                    k.note_w("attA", "s_sc", i)
                    k.note_w(f"attA_m{m}", "s_sc", i)
                ttpend.append(("scalar", f_te))
                self._p2_pend.append(ttpend)
            if p2sub == "att" or p2sub == "tt":
                continue
            while self._p2_pend:
                for engn, fn in self._p2_pend.pop(0):
                    self.on(engn, fn)
            for m in range(TPC):
                t = ch * TPC + m
                for oc2 in range(nhalf):
                    bank = (m * nhalf + oc2) % 2

                    def f_pj(eng, m=m, oc2=oc2, bank=bank, t=t):
                        k.wait_w(eng, "tensor", f"attA_m{m}")
                        k.wait_w(eng, "tensor", "wp_s")
                        k.wait_w(eng, "tensor", "bproj_s")
                        k.wait_w(eng, "tensor", "ones_row")
                        k.wait_rw(eng, "tensor", f"bank{bank}")
                        for ct in range(DT):
                            eng.matmul(
                                k.pb[bank].ap()[:, 0:hw],
                                k.attA.ap()[:, ct * CH + m * 128:
                                            ct * CH + (m + 1) * 128],
                                k.wp_s.ap()[:, ct * D + oc2 * hw:
                                            ct * D + (oc2 + 1) * hw],
                                start=(ct == 0), stop=False)
                        inst = eng.matmul(
                            k.pb[bank].ap()[:, 0:hw],
                            k.ones_row.ap(),
                            k.bproj_s.ap()[0:1, oc2 * hw:(oc2 + 1) * hw],
                            start=False, stop=True)
                        i = k.inc(inst, "s_pe")
                        k.note_w(f"bank{bank}", "s_pe", i)
                        k.note_r("attA", "s_pe", i)
                        k.note_r("wp_s", "s_pe", i)
                        k.note_r("bproj_s", "s_pe", i)
                    self.on("tensor", f_pj)

                    def f_pe2(eng, m=m, oc2=oc2, bank=bank, t=t):
                        k.wait_w(eng, "vector", f"bank{bank}")
                        k.wait_rw(eng, "vector", f"x{t}")
                        xsl = k.xres.ap()[:, t * D + oc2 * hw:
                                          t * D + (oc2 + 1) * hw]
                        i = k.inc(eng.tensor_tensor(
                            xsl, xsl, k.pb[bank].ap()[:, 0:hw], op=ALU.add), "s_v")
                        k.note_r(f"bank{bank}", "s_v", i)
                        k.note_w(f"x{t}", "s_v", i)
                    self.on("vector", f_pe2)

            self.emit_ln_stats_chunk(ch)
            self.emit_ln_chain(ch)
            self.emit_ln_n(ch, dst="kv")
            self._ln2_pend.extend(
                self.emit_ln_t(ch, ev="scalar", src="kv", defer=True))

        if cut == "p2":
            while self._ln2_pend:
                for engn, fn in self._ln2_pend.pop(0):
                    self.on(engn, fn)
            cut_dump(); return

        sb = {nm: (rlo, rhi) for nm, rlo, rhi in self._suite_bounds()}

        def _win_overlaps(e, names):
            lo, hi = e * ewsz, (e + 1) * ewsz
            return any(sb[nm][0] < hi and lo < sb[nm][1] for nm in names)

        for e in range(self.n_in_c2):
            if e not in self._we_loaded and \
                    not _win_overlaps(e, ["kv_s", "q_s", "attB", "kvaug"]):
                self.emit_expert_load(e, b, stage="B")

        self._x2_dumped = {}

        def f_dump(eng):
            for t in range(NT):
                k.wait_w(eng, "sync", f"x{t}")
                row0 = b * S + t * 128
                k.inc(eng.dma_start(
                    k.dr["out"][row0:row0 + 128, :],
                    k.xres.ap()[:, t * D:(t + 1) * D]), "s_st", 16)
            tot = k.cnt("s_st")
            for t in range(NT):
                k.note_r(f"x{t}", "s_st", tot)
                self._x2_dumped[t] = tot
        self.on("sync", f_dump)

        for e in range(self.n_in_c2):
            if e not in self._we_loaded and not _win_overlaps(e, ["kv_s"]):
                self.emit_expert_load(e, b, stage="C")
        for e in range(self.n_in_c2, E):
            self.emit_expert_load(e, b, stage="C")

        self.bind_we_aliases()
        self.merge_region("attA", "scoresT")
        self.merge_region("attA", "eb_s")

        def f_eb(eng):
            k.wait_rw(eng, "gpsimd", "eb_s")
            self.deps_w["eb_s"] = {}
            self.deps_r.pop("eb_s", None)
            k.inc(eng.dma_start(k.eb_s.ap(), k.dr["eb"]), "s_eb", 16)
            k.wait_rw(eng, "gpsimd", "sel8")
            self.deps_w["sel8"] = {}
            self.deps_r.pop("sel8", None)
            i = k.inc(eng.dma_start(k.sel8.ap(), k.dr["sel8"]), "s_eb", 16)
            k.note_w("eb_s", "s_eb", i)
            k.note_w("sel8", "s_eb", i)
        self.on("gpsimd", f_eb)



        self.merge_region("ktmp0", "gate")
        self.merge_region("ktmp1", "gate")
        for t in range(NT):
            ch = t // TPC
            bank = t % 2

            def f_g(eng, t=t, ch=ch, bank=bank):
                k.wait_w(eng, "tensor", f"lnxT{t}")
                k.wait_w(eng, "tensor", "gw_s")
                k.wait_w(eng, "tensor", "gb_s")
                k.wait_w(eng, "tensor", "ones_row")
                k.wait_rw(eng, "tensor", f"bank{bank}")
                for dt in range(DT):
                    eng.matmul(
                        k.pb[bank].ap()[:, 0:E],
                        k.lnx.ap()[:, dt * S + t * 128: dt * S + (t + 1) * 128],
                        k.gw_s.ap()[:, dt * E:(dt + 1) * E],
                        start=(dt == 0), stop=False)
                inst = eng.matmul(
                    k.pb[bank].ap()[:, 0:E], k.ones_row.ap(),
                    k.gb_s.ap()[0:1, 0:E], start=False, stop=True)
                i = k.inc(inst, "s_pe")
                k.note_w(f"bank{bank}", "s_pe", i)
                k.note_r(f"lnx{ch}", "s_pe", i)
                k.note_r("gw_s", "s_pe", i)
                k.note_r("gb_s", "s_pe", i)
            self.on("tensor", f_g)

            def f_ge(eng, t=t, bank=bank):
                k.wait_w(eng, "vector", f"bank{bank}")
                k.wait_rw(eng, "vector", "gate")
                i = k.inc(eng.tensor_copy(
                    k.lgraw.ap()[:, t * E:(t + 1) * E],
                    k.pb[bank].ap()[:, 0:E]), "s_v")
                k.note_r(f"bank{bank}", "s_v", i)
                k.note_w("gate", "s_v", i)
            self.on("vector", f_ge)
            if self._ln2_pend and t % 3 == 2:
                for engn, fn in self._ln2_pend.pop(0):
                    self.on(engn, fn)

        while self._ln2_pend:
            for engn, fn in self._ln2_pend.pop(0):
                self.on(engn, fn)

        for e in range(E):
            if e not in self._we_loaded:
                self.emit_expert_load(e, b, stage="D")
        self.flush_we_notes()

        def v3(ap):
            return ap.rearrange("p (t e) -> p t e", e=E)

        def f_t2(eng):
            k.wait_w(eng, "vector", "gate")
            k.inc(eng.tensor_reduce(
                k.m1.ap(), v3(k.lgraw.ap()), axis=AX.X, op=ALU.max), "s_v")
            k.wait(eng, "vector", "s_v", k.cnt("s_v"))
            k.inc(eng.tensor_tensor(
                v3(k.gt1.ap()), v3(k.lgraw.ap()),
                k.m1.ap().unsqueeze(2).broadcast_to([128, NT, E]),
                op=ALU.is_equal), "s_v")
            k.wait(eng, "vector", "s_v", k.cnt("s_v"))
            k.inc(eng.scalar_tensor_tensor(
                k.gt2.ap(), k.gt1.ap(), -1e30, k.lgraw.ap(),
                op0=ALU.mult, op1=ALU.add), "s_v")
            k.wait(eng, "vector", "s_v", k.cnt("s_v"))
            k.inc(eng.tensor_reduce(
                k.m2.ap(), v3(k.gt2.ap()), axis=AX.X, op=ALU.max), "s_v")
            k.wait(eng, "vector", "s_v", k.cnt("s_v"))
            i = k.inc(eng.tensor_tensor(
                k.ed.ap(), k.m2.ap(), k.m1.ap(), op=ALU.subtract), "s_v")
            k.note_w("gate", "s_v", i)
        self.on("vector", f_t2)

        def f_t2s(eng):
            k.wait_w(eng, "scalar", "gate")
            i = k.inc(eng.activation(k.ed.ap(), k.ed.ap(), AF.Exp), "s_sc")
            k.note_w("gate", "s_sc", i)
        self.on("scalar", f_t2s)

        def f_t2b(eng):
            k.wait_w(eng, "vector", "gate")
            k.inc(eng.tensor_scalar(
                k.rden.ap(), k.ed.ap(), scalar1=1.0, scalar2=None,
                op0=ALU.add), "s_v")
            k.wait(eng, "vector", "s_v", k.cnt("s_v"))
            k.inc(eng.reciprocal(k.rden.ap(), k.rden.ap()), "s_v")
            k.inc(eng.tensor_tensor(
                v3(k.gt1.ap()), v3(k.lgraw.ap()),
                k.m2.ap().unsqueeze(2).broadcast_to([128, NT, E]),
                op=ALU.is_ge), "s_v")
            i = k.inc(eng.tensor_tensor(
                v3(k.gt2.ap()), v3(k.lgraw.ap()),
                k.m1.ap().unsqueeze(2).broadcast_to([128, NT, E]),
                op=ALU.subtract), "s_v")
            k.note_w("gate", "s_v", i)
        self.on("vector", f_t2b)

        def f_t2e(eng):
            k.wait_w(eng, "scalar", "gate")
            i = k.inc(eng.activation(k.gt2.ap(), k.gt2.ap(), AF.Exp), "s_sc")
            k.note_w("gate", "s_sc", i)
        self.on("scalar", f_t2e)

        def f_t2f(eng):
            k.wait_w(eng, "vector", "gate")
            k.inc(eng.tensor_tensor(
                k.gt2.ap(), k.gt2.ap(), k.gt1.ap(), op=ALU.mult), "s_v")
            k.wait(eng, "vector", "s_v", k.cnt("s_v"))
            i = k.inc(eng.tensor_tensor(
                v3(k.scores.ap()), v3(k.gt2.ap()),
                k.rden.ap().unsqueeze(2).broadcast_to([128, NT, E]),
                op=ALU.mult), "s_v")
            k.note_w("gate", "s_v", i)
            k.note_w("scores_done", "s_v", i)
        self.on("vector", f_t2f)

        def f_sbf(eng):
            k.wait_w(eng, "vector", "gate")
            i = k.inc(eng.tensor_copy(
                k.scores_bf.ap(), k.scores.ap()), "s_v")
            k.note_w("gate", "s_v", i)
        self.on("vector", f_sbf)
        for t4 in range(0, NT, 4):
            def f_st(eng, t4=t4):
                k.wait_w(eng, "tensor", "gate")
                k.wait_w(eng, "tensor", "ident")
                k.wait_rw(eng, "tensor", "bank6")
                for j in range(min(4, NT - t4)):
                    t = t4 + j
                    inst = eng.transpose(
                        k.pbt[0].ap()[0:E, j * 128:(j + 1) * 128],
                        k.scores_bf.ap()[:, t * E:(t + 1) * E],
                        k.ident.ap())
                i = k.inc(inst, "s_pe")
                k.note_w("bank6", "s_pe", i)
                k.note_r("gate", "s_pe", i)
            self.on("tensor", f_st)

            def f_ste(eng, t4=t4):
                nj = min(4, NT - t4)
                k.wait_w(eng, "vector", "bank6")
                k.wait_rw(eng, "vector", "scoresT")
                i = k.inc(eng.tensor_copy(
                    k.scoresT.ap()[0:E, t4 * 128:t4 * 128 + nj * 128],
                    k.pbt[0].ap()[0:E, 0:nj * 128]), "s_v")
                k.note_r("bank6", "s_v", i)
                k.note_w("scoresT", "s_v", i)
                k.note_w(f"scoresT_g{t4 // 4}", "s_v", i)
            self.on("vector", f_ste)

        if cut == "gate":
            return

        def emit_z8(t):
            def f_z8(eng, t=t):
                k.wait_w(eng, "tensor", f"scoresT_g{t // 4}")
                k.wait_w(eng, "tensor", "sel8")
                for e in range(E):
                    bank = 4 + e // 4
                    if e % 4 == 0:
                        k.wait_rw(eng, "tensor", f"bank{bank}")
                    inst = eng.matmul(
                        k.pb[bank].ap()[:, (e % 4) * 128:(e % 4 + 1) * 128],
                        k.sel8.ap()[0:E, e * 128:(e + 1) * 128],
                        k.scoresT.ap()[0:E, t * 128:(t + 1) * 128],
                        start=True, stop=True)
                    if e % 4 == 3 or e == E - 1:
                        i = k.inc(inst, "s_pe")
                        k.note_w(f"bank{bank}", "s_pe", i)
                k.note_r("scoresT", "s_pe", k.cnt("s_pe"))
            self.on("tensor", f_z8)

        emit_z8(0)
        self._pend_evac = []
        for t in range(NT):
            ch = t // TPC

            def f_z8e(eng, t=t):
                k.wait_rw(eng, "vector", "z8")
                nb = max(1, E // 4)
                for j in range(nb):
                    bank = 4 + j
                    k.wait_w(eng, "vector", f"bank{bank}")
                    nn = min(4, E - j * 4) * 128
                    i = k.inc(eng.tensor_copy(
                        k.z8.ap()[:, j * 512:j * 512 + nn],
                        k.pb[bank].ap()[:, 0:nn]), "s_v")
                    k.note_r(f"bank{bank}", "s_v", i)
                k.note_w("z8", "s_v", i)
            self.on("vector", f_z8e)

            for half in (0, 1):
                lhname = "lhsT_lo" if half == 0 else "lhsT_hi"

                def f_mk(eng, t=t, ch=ch, half=half, lhname=lhname):
                    dst = getattr(k, lhname)
                    k.wait_w(eng, "vector", "z8")
                    k.wait_w(eng, "vector", f"lnxT{t}")
                    k.wait_rw(eng, "vector", lhname)
                    in0 = k.lnx.ap().rearrange("p (dt s) -> p dt s", dt=DT)[
                        :, :, t * 128:(t + 1) * 128].unsqueeze(1).broadcast_to(
                        [128, EH, DT, 128])
                    in1 = k.z8.ap()[:, half * EH * 128:(half + 1) * EH * 128
                                    ].rearrange("p (e n) -> p e n", e=EH
                                                ).unsqueeze(2).broadcast_to(
                        [128, EH, DT, 128])
                    i = k.inc(eng.tensor_tensor(
                        dst.ap().rearrange("p (e dt n) -> p e dt n",
                                           e=EH, dt=DT),
                        in0, in1, op=ALU.mult), "s_v")
                    k.note_w(lhname, "s_v", i)
                    k.note_r(f"lnx{ch}", "s_v", i)
                    k.note_r("z8", "s_v", i)
                self.on("vector", f_mk)

            if t + 1 < NT:
                emit_z8(t + 1)

            banks = [(t % 2) * 2 + oc for oc in range(NOC)]

            def f_em(eng, t=t, banks=banks):
                k.wait_w(eng, "tensor", f"scoresT_g{t // 4}")
                k.wait_w(eng, "tensor", "eb_s")
                for oc in range(NOC):
                    bank = banks[oc]
                    k.wait_rw(eng, "tensor", f"bank{bank}")
                    eng.matmul(
                        k.pb[bank].ap()[:, 0:OC],
                        k.scoresT.ap()[0:E, t * 128:(t + 1) * 128],
                        k.eb_s.ap()[0:E, oc * OC:(oc + 1) * OC],
                        start=True, stop=False)
                eorder = (list(range(self.n_in_c2, E))
                          + list(range(self.n_in_c2)))
                half_left = {0: sum(1 for e in eorder if e < EH),
                             1: sum(1 for e in eorder if e >= EH)}
                for idx, e in enumerate(eorder):
                    half = e // EH
                    lhname = "lhsT_lo" if half == 0 else "lhsT_hi"
                    lh = getattr(k, lhname)
                    k.wait_w(eng, "tensor", lhname)
                    k.wait_w(eng, "tensor", f"we{e}")
                    last_inst = None
                    for oc in range(NOC):
                        bank = banks[oc]
                        for dt in range(DT):
                            last_inst = eng.matmul(
                                k.pb[bank].ap()[:, 0:OC],
                                lh.ap()[:, ((e % EH) * DT + dt) * 128:
                                        ((e % EH) * DT + dt + 1) * 128],
                                k.we[e].ap()[:, dt * O + oc * OC:
                                             dt * O + (oc + 1) * OC],
                                start=False,
                                stop=(idx == E - 1 and dt == DT - 1))
                    i = k.inc(last_inst, "s_pe")
                    k.note_r(f"we{e}", "s_pe", i)
                    half_left[half] -= 1
                    if half_left[half] == 0:
                        k.note_r(lhname, "s_pe", i)
                    if idx == E - 1:
                        for bank in banks:
                            k.note_w(f"bank{bank}", "s_pe", i)
                        k.note_r("scoresT", "s_pe", i)
                        k.note_r("eb_s", "s_pe", i)
            self.on("tensor", f_em)

            pend = []
            for oc in range(NOC):
                bank = banks[oc]
                pidx = (t * NOC + oc) % 2
                ot = self.otmp0 if pidx == 0 else self.otmp1
                otname = f"otmp{pidx}"

                def f_oe(eng, bank=bank, ot=ot, otname=otname):
                    k.wait_w(eng, "vector", f"bank{bank}")
                    k.wait_rw(eng, "vector", otname)
                    i = k.inc(eng.tensor_copy(
                        ot.ap(), k.pb[bank].ap()[:, 0:OC]), "s_v")
                    k.note_r(f"bank{bank}", "s_v", i)
                    k.note_w(otname, "s_v", i)
                pend.append(("vector", f_oe))

                def f_oa(eng, t=t, oc=oc, ot=ot, otname=otname, pidx=pidx):
                    k.wait_w(eng, "gpsimd", otname)
                    k.wait(eng, "gpsimd", "s_st", self._x2_dumped[t])
                    row0 = b * S + t * 128
                    i = k.inc(eng.dma_start(
                        k.dr["out"][row0:row0 + 128, oc * OC:(oc + 1) * OC],
                        ot.ap(), accum_op=ALU.add), f"s_ac{pidx}", 16)
                    k.note_r(otname, f"s_ac{pidx}", i)
                pend.append(("gpsimd", f_oa))
            self._pend_evac.append(pend)
            while len(self._pend_evac) > 1:
                for engn, fn in self._pend_evac.pop(0):
                    self.on(engn, fn)

        while self._pend_evac:
            for engn, fn in self._pend_evac.pop(0):
                self.on(engn, fn)
        if b == 0:
            self.emit_scratch_we()
        spanmap = {nm: (lo, hi) for nm, lo, hi in self.affn_spans}
        chb = TPC * D * 4
        for r in ["lhsT_lo", "lhsT_hi", "z8", "otmp0", "otmp1", "sel8"] + \
                 [f"we{e}" for e in range(self.n_in_c2, E)]:
            lo, hi = spanmap[r]
            for ch2 in range(lo // chb, min(NCH, (hi + chb - 1) // chb)):
                self.merge_region(r, f"ffnA{ch2}")
        self.merge_region("gate", "ktmp0")
        self.merge_region("gate", "ktmp1")
        ewsz = DT * O * 2
        suite = [("wkv_s", DT * 2 * D * 2), ("wq_s", DT * D * 2),
                 ("wp_s", DT * D * 2), ("kv_s", TPC * 2 * D * 2),
                 ("q_s", DT * CH * 2), ("attB", 2 * D * 2),
                 ("kvaug", (NPAIR * 130 * 2 + 31) & ~31)]
        pos = 0
        sbounds = []
        for nm, sz in suite:
            sbounds.append((nm, pos, pos + sz))
            pos += sz
        for e in range(self.n_in_c2):
            lo, hi = e * ewsz, (e + 1) * ewsz
            for nm, rlo, rhi in sbounds:
                if rlo < hi and lo < rhi:
                    self.merge_region(f"we{e}", nm)
        self.merge_region("scoresT", "attA")
        self.merge_region("eb_s", "attA")
        if b + 1 < B:
            self.emit_weight_loads(b + 1)

    def emit_dbg(self):
        k = self
        c = self.cfg

        def f(eng):
            col = [0]

            def dump(ap, n):
                k.inc(eng.dma_start(
                    k.dr["dbg"][0:ap.shape[0], col[0]:col[0] + n], ap),
                    "s_st", 16)
                col[0] += n
            for s in ["s_v", "s_sc", "s_pe"]:
                k.wait(eng, "gpsimd", s, k.cnt(s))
            dump(k.scores.ap(), c["NT"] * c["E"])
            dump(k.scoresT.ap(), c["S"])
            dump(k.z8.ap(), c["E"] * 128)
            dump(k.lhsT_lo.ap(), (c["E"] // 2) * c["DT"] * 128)
            dump(k.lhsT_hi.ap(), (c["E"] // 2) * c["DT"] * 128)
            dump(k.lnx.ap()[:, 0:2048], 2048)
            dump(k.kvaug.ap(), c["NPAIR"] * 130)
        self.on("gpsimd", f)

    def emit_tail(self):
        if self.cfg.get("DBG"):
            self.emit_dbg()
        k = self

        def f(eng):
            for s in (["s_ac0", "s_ac1", "s_wl", "s_st", "s_eb"]
                      + [f"s_we{st}" for st in "ABCD"]
                      + [f"s_ws{st}" for st in "ABCD"]):
                if k.cnt(s):
                    eng.wait_ge(k.sems[s], k.cnt(s))
        self.on("gpsimd", f)

        def f2(eng):
            if k.cnt("s_st"):
                eng.wait_ge(k.sems["s_st"], k.cnt("s_st"))
        self.on("sync", f2)


def prep_inputs(cfg, x, W_qkv, W_proj, b_proj, gate_w, gate_b, expert_w,
                expert_b):
    import ml_dtypes
    D = cfg["D"]
    m = {}
    m["x"] = np.ascontiguousarray(x.reshape(-1, D).astype(np.float32))
    m["wkvT"] = np.ascontiguousarray(np.asarray(W_qkv)[D:3 * D, :].T)
    m["wqT"] = np.ascontiguousarray(np.asarray(W_qkv)[0:D, :].T)
    m["wpT"] = np.ascontiguousarray(np.asarray(W_proj).T)
    m["bproj"] = np.ascontiguousarray(np.asarray(b_proj).reshape(1, D))
    m["gwT"] = np.ascontiguousarray(np.asarray(gate_w).T)
    m["gb"] = np.ascontiguousarray(np.asarray(gate_b).reshape(1, -1))
    m["ew"] = np.ascontiguousarray(np.asarray(expert_w))
    m["eb"] = np.ascontiguousarray(np.asarray(expert_b))
    m["ident"] = np.eye(128, dtype=ml_dtypes.bfloat16)
    m["ones_row"] = np.ones((1, 128), ml_dtypes.bfloat16)
    E = m["eb"].shape[0]
    sel = np.zeros((E, E * 128), np.float32)
    for e in range(E):
        sel[e, e * 128:(e + 1) * 128] = 1.0
    m["sel8"] = sel.astype(ml_dtypes.bfloat16)
    m["ones_col"] = np.ones((128, 1), ml_dtypes.bfloat16)
    return m






from concourse.bass_utils import run_bass_kernel_spmd

_NC_CACHE = {}




from concourse.bass_utils import run_bass_kernel_spmd

_NC_CACHE = {}


def _get_program():
    if "nc" not in _NC_CACHE:
        bl = Builder(default_cfg())
        _NC_CACHE["nc"] = bl.build()
    return _NC_CACHE["nc"]


def kernel(L0_attn_x, L0_attn_W_qkv, L0_attn_W_proj, L0_attn_b_proj,
           L0_ffn_gate_w, L0_ffn_gate_b, L0_ffn_expert_w, L0_ffn_expert_b):
    cfg = default_cfg()
    n_cores = 8
    B_total = 32
    Bc = B_total // n_cores
    x = np.ascontiguousarray(np.asarray(L0_attn_x, dtype=np.float32))
    nc = _get_program()
    base = prep_inputs(cfg, x[0:Bc],
                       np.asarray(L0_attn_W_qkv, np.float32),
                       np.asarray(L0_attn_W_proj, np.float32),
                       np.asarray(L0_attn_b_proj, np.float32),
                       np.asarray(L0_ffn_gate_w, np.float32),
                       np.asarray(L0_ffn_gate_b, np.float32),
                       np.asarray(L0_ffn_expert_w, np.float32),
                       np.asarray(L0_ffn_expert_b, np.float32))
    in_maps = []
    for c in range(n_cores):
        m = dict(base)
        m["x"] = np.ascontiguousarray(
            x[c * Bc:(c + 1) * Bc].reshape(Bc * cfg["S"], cfg["D"]))
        in_maps.append(m)
    res = run_bass_kernel_spmd(nc, in_maps, core_ids=list(range(n_cores)))
    out = np.concatenate(
        [np.asarray(r["out"]).reshape(Bc, cfg["S"], cfg["D"])
         for r in res.results], axis=0)
    return out.astype(np.float32)


# version 25 (final)
# speedup vs baseline: 1.0353x; 1.0353x over previous
import numpy as np
import concourse.bass as bass
import concourse.mybir as mybir

F32 = mybir.dt.float32
BF16 = mybir.dt.bfloat16
AF = mybir.ActivationFunctionType
ALU = mybir.AluOpType
AX = mybir.AxisListType

SBUF_BYTES = 229376
SBUF_BASE = 16384 + 512


def default_cfg():
    return dict(B=4, S=2048, D=1024, H=16, E=8, O=1024, CH=512, FC=512, OC=512,
                EPS=1e-5, ZEPS=1e-6)


def mini_cfg():
    return dict(B=2, S=1280, D=256, H=4, E=8, O=256, CH=256, FC=256, OC=128,
                EPS=1e-5, ZEPS=1e-6)


class Builder:
    def __init__(self, cfg):
        self.cfg = dict(cfg)
        c = self.cfg
        c["DT"] = c["D"] // 128
        c["NT"] = c["S"] // 128
        c["NCH"] = c["S"] // c["CH"]
        c["TPC"] = c["CH"] // 128
        c["NPAIR"] = c["H"] // 2
        c["NFC"] = (2 * c["D"]) // c["FC"]
        c["NOC"] = c["O"] // c["OC"]
        assert c["H"] * 64 == c["D"]
        self.sems = {}
        self.waited = {}
        self.counts = {}
        self.deps_w = {}
        self.deps_r = {}

    def _merge(self, d, sem, val):
        if d.get(sem, -1) < val:
            d[sem] = val

    def note_w(self, region, sem, val):
        self._merge(self.deps_w.setdefault(region, {}), sem, val)

    def note_r(self, region, sem, val):
        self._merge(self.deps_r.setdefault(region, {}), sem, val)

    def merge_region(self, src, dst, into_r=True):
        for s, v in self.deps_w.get(src, {}).items():
            (self.note_r if into_r else self.note_w)(dst, s, v)
        for s, v in self.deps_r.get(src, {}).items():
            self.note_r(dst, s, v)

    def wait_w(self, eng, engname, region):
        for s, v in self.deps_w.get(region, {}).items():
            self.wait(eng, engname, s, v)

    def wait_rw(self, eng, engname, region):
        self.wait_w(eng, engname, region)
        for s, v in self.deps_r.get(region, {}).items():
            self.wait(eng, engname, s, v)

    def wait(self, eng, engname, semname, val):
        if val <= 0:
            return
        key = (engname, semname)
        if self.waited.get(key, 0) >= val:
            return
        eng.wait_ge(self.sems[semname], val)
        self.waited[key] = val

    def inc(self, inst, semname, k=1):
        inst.then_inc(self.sems[semname], k)
        self.counts[semname] = self.counts.get(semname, 0) + k
        return self.counts[semname]

    def cnt(self, semname):
        return self.counts.get(semname, 0)

    def on(self, engname, fn):
        fn(self._recs[engname])

    class _Inst:
        __slots__ = ("rec",)

        def __init__(self, rec):
            self.rec = rec

        def then_inc(self, sem, k):
            self.rec.append(("__then_inc", (sem, k), {}))
            return self

    class _Rec:

        def __init__(self, outer):
            self.calls = []
            self.outer = outer

        def __getattr__(self, name):
            calls = self.calls

            def method(*a, **kw):
                calls.append((name, a, kw))
                return Builder._Inst(calls)
            return method

        def append(self, x):
            self.calls.append(x)

    def flush_sections(self):
        for engname, rec in self._recs.items():
            calls = rec.calls
            if not calls:
                continue

            def replay(eng, calls=calls):
                last = None
                for name, a, kw in calls:
                    if name == "__then_inc":
                        last.then_inc(*a)
                    else:
                        last = getattr(eng, name)(*a, **kw)
            getattr(self.blk, engname)(replay)

    def build(self):
        c = self.cfg
        nc = bass.Bass("TRN2", target_bir_lowering=False,
                       dynamic_dma_scratch_size=16384)
        self.nc = nc
        B, S, D, H, E, O = c["B"], c["S"], c["D"], c["H"], c["E"], c["O"]
        CH, FC, OC = c["CH"], c["FC"], c["OC"]
        DT, NT, NCH, TPC = c["DT"], c["NT"], c["NCH"], c["TPC"]
        NPAIR = c["NPAIR"]

        dr = {}
        dr["x"] = nc.dram_tensor("x", [B * S, D], F32, kind="ExternalInput").ap()
        dr["wkvT"] = nc.dram_tensor("wkvT", [D, 2 * D], F32, kind="ExternalInput").ap()
        dr["wqT"] = nc.dram_tensor("wqT", [D, D], F32, kind="ExternalInput").ap()
        dr["wpT"] = nc.dram_tensor("wpT", [D, D], F32, kind="ExternalInput").ap()
        dr["bproj"] = nc.dram_tensor("bproj", [1, D], F32, kind="ExternalInput").ap()
        dr["gwT"] = nc.dram_tensor("gwT", [D, E], F32, kind="ExternalInput").ap()
        dr["gb"] = nc.dram_tensor("gb", [1, E], F32, kind="ExternalInput").ap()
        dr["ew"] = nc.dram_tensor("ew", [E, D, O], F32, kind="ExternalInput").ap()
        dr["eb"] = nc.dram_tensor("eb", [E, O], F32, kind="ExternalInput").ap()
        dr["ident"] = nc.dram_tensor("ident", [128, 128], BF16, kind="ExternalInput").ap()
        dr["ones_row"] = nc.dram_tensor("ones_row", [1, 128], BF16, kind="ExternalInput").ap()
        dr["ones_col"] = nc.dram_tensor("ones_col", [128, 1], BF16, kind="ExternalInput").ap()
        dr["sel8"] = nc.dram_tensor("sel8", [E, E * 128], BF16, kind="ExternalInput").ap()
        dr["out"] = nc.dram_tensor("out", [B * S, D], F32, kind="ExternalOutput").ap()
        dr["scr_wkv"] = nc.dram_tensor("scr_wkv", [128, DT * 2 * D], BF16,
                                       kind="Internal").ap()
        dr["scr_wq"] = nc.dram_tensor("scr_wq", [128, DT * D], BF16,
                                      kind="Internal").ap()
        dr["scr_wp"] = nc.dram_tensor("scr_wp", [128, DT * D], BF16,
                                      kind="Internal").ap()
        for e in range(E):
            dr[f"scr_we{e}"] = nc.dram_tensor(
                f"scr_we{e}", [128, DT * O], BF16, kind="Internal").ap()
        if c.get("DBG"):
            dr["dbg"] = nc.dram_tensor("dbg", [128, 16384], F32,
                                       kind="ExternalOutput").ap()
        self.dr = dr

        off = SBUF_BASE

        self._manual_names = set()

        def at(name, shape, dtype, offset):
            assert offset % 32 == 0, (name, offset)
            self._manual_names.add(name)
            return nc.alloc_sbuf_tensor_at(name, shape, dtype, offset=offset)

        def bump(n):
            return (n + 31) & ~31

        a0 = off
        xres = at("xres", [128, NT * D], F32, off); off = bump(off + (NT * D * 4))
        lnx = at("lnx", [128, DT * S], BF16, off); off = bump(off + (DT * S * 2))
        ident = at("ident", [128, 128], BF16, off); off = bump(off + (256))
        ones_row = at("ones_row", [1, 128], BF16, off); off = bump(off + (256))
        ones_col = at("ones_col", [128, 1], BF16, off); off = bump(off + (32))
        bn6 = at("bn6", [128, NT * 12], F32, off); off = bump(off + (NT * 48))
        st2 = at("st2", [128, NT * 2], F32, off); off = bump(off + (NT * 8))
        mean = at("mean", [128, NT], F32, off); off = bump(off + (NT * 4))
        varc = at("varc", [128, NT], F32, off); off = bump(off + (NT * 4))
        rstd = at("rstd", [128, NT], F32, off); off = bump(off + (NT * 4))
        negm = at("negm", [128, NT], F32, off); off = bump(off + (NT * 4))
        mtmp = at("mtmp", [128, NT], F32, off); off = bump(off + (NT * 4))
        scores = at("scores", [128, NT * E], F32, off); off = bump(off + (NT * E * 4))
        bproj_s = at("bproj_s", [1, D], BF16, off); off = bump(off + (D * 2))
        gb_s = at("gb_s", [1, E], BF16, off); off = bump(off + (max(E * 2, 32)))
        gw_s = at("gw_s", [128, DT * E], BF16, off); off = bump(off + (DT * E * 2))
        r_all = at("r_all", [128, H], F32, off); off = bump(off + (H * 4))
        ktw = bump(max(FC, CH) * 2)
        gate_need = 3 * bump(NT * E * 4) + 4 * bump(NT * 4) + bump(NT * E * 2)
        kt0 = off
        ktmp0 = at("ktmp0", [128, max(FC, CH)], BF16, kt0)
        ktmp1 = at("ktmp1", [128, max(FC, CH)], BF16, kt0 + ktw)
        off = bump(kt0 + max(2 * ktw, gate_need))
        g = kt0
        lgraw = at("lgraw", [128, NT * E], F32, g); g = bump(g + NT * E * 4)
        gt1 = at("gt1", [128, NT * E], F32, g); g = bump(g + NT * E * 4)
        gt2 = at("gt2", [128, NT * E], F32, g); g = bump(g + NT * E * 4)
        m1 = at("m1", [128, NT], F32, g); g = bump(g + NT * 4)
        m2 = at("m2", [128, NT], F32, g); g = bump(g + NT * 4)
        ed = at("ed", [128, NT], F32, g); g = bump(g + NT * 4)
        rden = at("rden", [128, NT], F32, g); g = bump(g + NT * 4)
        scores_bf = at("scores_bf", [128, NT * E], BF16, g); g = bump(g + NT * E * 2)
        assert g <= kt0 + max(2 * ktw, gate_need), \
            f"gate tmps overflow ktmp window {g - kt0}"

        c2 = off
        sz_wkv = DT * 2 * D * 2
        sz_wq = DT * D * 2
        sz_wp = DT * D * 2
        sz_kv = TPC * 2 * D * 2
        sz_q = DT * CH * 2
        sz_attB = 2 * D * 2
        sz_kvaug = max(bump(NPAIR * 130 * 2), 4096)
        wkv_s = at("wkv_s", [128, DT * 2 * D], BF16, c2)
        wq_s = at("wq_s", [128, DT * D], BF16, c2 + sz_wkv)
        wp_s = at("wp_s", [128, DT * D], BF16, c2 + sz_wkv + sz_wq)
        kv_s = at("kv_s", [128, TPC * 2 * D], BF16, c2 + sz_wkv + sz_wq + sz_wp)
        q_s = at("q_s", [128, DT * CH], BF16, c2 + sz_wkv + sz_wq + sz_wp + sz_kv)
        attB = at("attB", [128, 2 * D], BF16,
                  c2 + sz_wkv + sz_wq + sz_wp + sz_kv + sz_q)
        kvaug = at("kvaug", [128, NPAIR * 130], BF16,
                   c2 + sz_wkv + sz_wq + sz_wp + sz_kv + sz_q + sz_attB)
        c2_end = (c2 + sz_wkv + sz_wq + sz_wp + sz_kv + sz_q + sz_attB
                  + sz_kvaug)
        ewsz = DT * O * 2
        n_in_c2 = min(E, (c2_end - c2) // ewsz, c.get("NC2", E))
        self.n_in_c2 = n_in_c2
        we = [at(f"we{e}", [128, DT * O], BF16, c2 + e * ewsz)
              for e in range(n_in_c2)]
        attA = at("attA", [128, DT * CH], BF16, c2_end)
        scoresT = at("scoresT", [E, S], BF16, c2_end)
        eb_s = at("eb_s", [E, O], BF16, c2_end + S * 2)
        kvacc = at("kvacc", [128, NPAIR * 65], F32, c2_end)
        tail_p1 = bump(NPAIR * 65 * 4)
        tail_end = c2_end + max(DT * CH * 2, S * 2 + O * 2, tail_p1)
        aoff = a0
        self.affn_spans = []
        for e in range(n_in_c2, E):
            we.append(at(f"we{e}", [128, DT * O], BF16, aoff))
            self.affn_spans.append((f"we{e}", aoff - a0, aoff - a0 + ewsz))
            aoff += ewsz
        EH = E // 2
        lhsT_lo = at("lhsT_lo", [128, EH * DT * 128], BF16, aoff)
        self.affn_spans.append(("lhsT_lo", aoff - a0, aoff - a0 + EH * DT * 256))
        aoff += EH * DT * 128 * 2
        lhsT_hi = at("lhsT_hi", [128, EH * DT * 128], BF16, aoff)
        self.affn_spans.append(("lhsT_hi", aoff - a0, aoff - a0 + EH * DT * 256))
        aoff += EH * DT * 128 * 2
        z8 = at("z8", [128, E * 128], BF16, aoff)
        self.affn_spans.append(("z8", aoff - a0, aoff - a0 + E * 256))
        aoff += E * 128 * 2
        otmp0 = at("otmp0", [128, OC], F32, aoff)
        self.affn_spans.append(("otmp0", aoff - a0, aoff - a0 + OC * 4))
        aoff += OC * 4
        otmp1 = at("otmp1", [128, OC], F32, aoff)
        self.affn_spans.append(("otmp1", aoff - a0, aoff - a0 + OC * 4))
        aoff += OC * 4
        sel8 = at("sel8", [E, E * 128], BF16, aoff)
        self.affn_spans.append(("sel8", aoff - a0, aoff - a0 + E * 256))
        aoff += E * 128 * 2
        assert aoff <= a0 + NT * D * 4, f"ffn-alias overflow {aoff - a0}"
        assert tail_end <= SBUF_BYTES, f"SBUF overflow {tail_end}"
        self.sbuf_used = tail_end

        pb = [nc.alloc_psum_tensor(f"pb{i}", [128, 512], F32) for i in range(6)]
        pbt = [nc.alloc_psum_tensor(f"pbt{i}", [128, 1024], BF16) for i in range(2)]

        semnames = (["s_pe", "s_v", "s_sc", "s_c", "s_wl", "s_st",
                     "s_ac0", "s_ac1", "s_scr", "s_scrk", "s_wls",
                     "s_eb"]
                    + [f"s_we{st}" for st in "ABCD"]
                    + [f"s_ws{st}" for st in "ABCD"]
                    + [f"s_x{ch}" for ch in range(NCH)]
                    + [f"s_wkv{fc}" for fc in range(c["NFC"])]
                    + [f"s_wkvg{fc}" for fc in range(c["NFC"])])
        for name in semnames:
            self.sems[name] = nc.alloc_semaphore(name)

        ns = locals()
        for nm in ["xres", "lnx", "ident", "ones_row", "ones_col", "scores_bf",
                   "bn6", "st2", "mean", "varc", "rstd", "negm", "mtmp",
                   "lgraw", "scores", "gt1", "gt2", "m1", "m2", "ed", "rden",
                   "bproj_s", "gb_s", "gw_s", "kvaug", "r_all",
                   "kvacc",
                   "ktmp0", "ktmp1", "wkv_s", "wq_s", "wp_s", "kv_s", "q_s",
                   "attB", "attA", "scoresT", "eb_s", "we", "lhsT_lo", "sel8",
                   "lhsT_hi", "z8", "otmp0", "otmp1", "pb", "pbt"]:
            setattr(self, nm, ns[nm])

        self.blk = nc.Block()
        self.blk.__enter__()
        self._recs = {e: Builder._Rec(self) for e in
                      ["sync", "gpsimd", "tensor", "vector", "scalar"]}
        self.emit_consts()
        for b in range(B):
            self.emit_batch(b)
        self.emit_tail()
        self.flush_sections()
        self.blk.__exit__(None, None, None)
        self.check_auto_allocs()
        return nc

    def check_auto_allocs(self):
        import concourse.mybir as mb
        for a in self.nc.m.functions[0].allocations:
            if not isinstance(a, mb.MemoryLocationSet):
                continue
            ml = a.memorylocations[0]
            if getattr(ml, "type", None) != "SB":
                continue
            if ml.name.startswith(tuple(self._manual_names)):
                continue
            addr = getattr(ml, "addr", None)
            if addr is None:
                continue
            import concourse.mybir as _mb
            dims = list(ml.dims)
            dt = ml.dtype
            dsz = _mb.dt.size(dt) if dt is not None else 1
            per_part = int(np.prod(dims[1:])) * dsz if len(dims) > 1 else dsz
            end = addr + per_part
            assert end <= SBUF_BASE or addr >= SBUF_BYTES, \
                f"auto alloc {ml.name} at {addr}..{end} collides with manual region"

    def emit_consts(self):
        k = self

        def f(eng):
            k.inc(eng.dma_start(k.ident.ap(), k.dr["ident"]), "s_c", 16)
            k.inc(eng.dma_start(k.ones_row.ap(), k.dr["ones_row"]), "s_c", 16)
            k.inc(eng.dma_start(k.ones_col.ap(), k.dr["ones_col"]), "s_c", 16)
            tot = k.cnt("s_c")
            for r in ["ident", "ones_row", "ones_col"]:
                k.note_w(r, "s_c", tot)
        self.on("sync", f)

    def emit_weight_loads(self, b):
        k = self
        c = self.cfg
        DT, D, FC = c["DT"], c["D"], c["FC"]
        NFC = c["NFC"]

        def f(eng):
            regs = ["wkv_s", "wq_s", "wp_s", "gw_s", "bproj_s", "gb_s"]
            for r in regs:
                k.wait_rw(eng, "gpsimd" if b == 0 else "sync", r)
            for r in regs:
                self.deps_w[r] = {}
                self.deps_r.pop(r, None)
            if b == 0:
                for fc in range(NFC):
                    vw = k.wkv_s.ap().rearrange(
                        "p (t n) -> p t n", t=DT)[:, :, fc * FC:(fc + 1) * FC]
                    vs = k.dr["wkvT"].rearrange(
                        "(t p) n -> p t n", p=128)[:, :, fc * FC:(fc + 1) * FC]
                    i = k.inc(eng.dma_start(vw, vs), f"s_wkvg{fc}", 16)
                    k.note_w(f"wkv_fc{fc}", f"s_wkvg{fc}", i)
                    k.note_w("wkv_s", f"s_wkvg{fc}", i)
                for r, dsrc, dst, nt in [("wq_s", "wqT", k.wq_s, DT),
                                         ("wp_s", "wpT", k.wp_s, DT)]:
                    k.inc(eng.dma_start(
                        dst.ap().rearrange("p (t n) -> p t n", t=nt),
                        k.dr[dsrc].rearrange("(t p) n -> p t n", p=128)),
                        "s_wl", 16)
        if b == 0:
            self.on("gpsimd", f)
        else:
            def fqp(eng):
                for r in ["wq_s", "wp_s"]:
                    k.wait_rw(eng, "scalar", r)
                for r in ["wq_s", "wp_s"]:
                    self.deps_w[r] = {}
                    self.deps_r.pop(r, None)
                k.wait(eng, "scalar", "s_scr", self._scr_total)
                k.inc(eng.dma_start(k.wq_s.ap(), k.dr["scr_wq"]), "s_wls", 16)
                i = k.inc(eng.dma_start(k.wp_s.ap(), k.dr["scr_wp"]), "s_wls", 16)
                k.note_w("wq_s", "s_wls", i)
                k.note_w("wp_s", "s_wls", i)
            self.on("scalar", fqp)

        def f2(eng):
            for r in ["gw_s", "bproj_s", "gb_s"]:
                k.wait_rw(eng, "gpsimd", r)
            for r, dsrc, dst, nt in [("gw_s", "gwT", k.gw_s, DT)]:
                k.inc(eng.dma_start(
                    dst.ap().rearrange("p (t n) -> p t n", t=nt),
                    k.dr[dsrc].rearrange("(t p) n -> p t n", p=128)), "s_wl", 16)
            for r, dsrc, dst in [("bproj_s", "bproj", k.bproj_s),
                                 ("gb_s", "gb", k.gb_s)]:
                k.inc(eng.dma_start(dst.ap(), k.dr[dsrc]), "s_wl", 16)
            tot = k.cnt("s_wl")
            for r in ["gw_s", "bproj_s", "gb_s", "wq_s", "wp_s"]:
                k.note_w(r, "s_wl", tot)
        self.on("gpsimd", f2)

    def emit_wkv_reload(self, fc):
        k = self
        c = self.cfg
        DT, FC = c["DT"], c["FC"]

        def f(eng, fc=fc):
            if fc == 0:
                k.wait_rw(eng, "sync", "wkv_s")
                self.deps_w["wkv_s"] = {}
                self.deps_r.pop("wkv_s", None)
            k.wait(eng, "sync", "s_scrk", 16)
            vw = k.wkv_s.ap().rearrange(
                "p (t n) -> p t n", t=DT)[:, :, fc * FC:(fc + 1) * FC]
            vs = k.dr["scr_wkv"].rearrange(
                "p (t n) -> p t n", t=DT)[:, :, fc * FC:(fc + 1) * FC]
            i = k.inc(eng.dma_start(vw, vs), f"s_wkv{fc}", 16)
            k.note_w(f"wkv_fc{fc}", f"s_wkv{fc}", i)
            k.note_w("wkv_s", f"s_wkv{fc}", i)
        self.on("sync", f)

    def emit_expert_load(self, e, b, stage="D"):
        k = self
        DT = self.cfg["DT"]
        self.bind_we_span(e)
        self._we_loaded.add(e)
        if b == 0:
            sem = f"s_we{stage}"

            def f(eng):
                k.wait_rw(eng, "gpsimd", f"we{e}")
                self.deps_w[f"we{e}"] = {}
                self.deps_r.pop(f"we{e}", None)
                k.wait(eng, "gpsimd", "s_scr", self._scr_total)
                i = k.inc(eng.dma_start(
                    k.we[e].ap().rearrange("p (t n) -> p t n", t=DT),
                    k.dr["ew"][e].rearrange("(t p) n -> p t n", p=128)),
                    sem, 16)
                self._we_pend.append((e, sem))
            self.on("gpsimd", f)
        else:
            sem = f"s_ws{stage}"

            def f(eng):
                k.wait_rw(eng, "sync", f"we{e}")
                self.deps_w[f"we{e}"] = {}
                self.deps_r.pop(f"we{e}", None)
                k.wait(eng, "sync", "s_scr", self._scr_total)
                i = k.inc(eng.dma_start(
                    k.we[e].ap(), k.dr[f"scr_we{e}"]), sem, 16)
                self._we_pend.append((e, sem))
            self.on("sync", f)

    def flush_we_notes(self):
        for e, sem in self._we_pend:
            self.note_w(f"we{e}", sem, self.cnt(sem))
        self._we_pend = []

    def emit_scratch_w_qkv(self):
        k = self

        def f(eng):
            for fc in range(self.cfg["NFC"]):
                k.wait_w(eng, "scalar", f"wkv_fc{fc}")
            k.wait_w(eng, "scalar", "wq_s")
            k.wait_w(eng, "scalar", "wp_s")
            i = k.inc(eng.dma_start(k.dr["scr_wkv"], k.wkv_s.ap()),
                      "s_scrk", 16)
            k.note_r("wkv_s", "s_scrk", i)
            k.inc(eng.dma_start(k.dr["scr_wq"], k.wq_s.ap()), "s_scr", 16)
            k.inc(eng.dma_start(k.dr["scr_wp"], k.wp_s.ap()), "s_scr", 16)
            t = k.cnt("s_scr")
            for r in ["wq_s", "wp_s"]:
                k.note_r(r, "s_scr", t)
        self.on("scalar", f)

    def emit_scratch_we(self):
        k = self

        def f(eng):
            for st in "ABCD":
                k.wait(eng, "scalar", f"s_we{st}", k.cnt(f"s_we{st}"))
            for e in range(self.cfg["E"]):
                k.inc(eng.dma_start(
                    k.dr[f"scr_we{e}"], k.we[e].ap()), "s_scr", 16)
            self._scr_total = k.cnt("s_scr")
            for e in range(self.cfg["E"]):
                k.note_r(f"we{e}", "s_scr", self._scr_total)
        self.on("scalar", f)

    def _suite_bounds(self):
        c = self.cfg
        suite = [("wkv_s", c["DT"] * 2 * c["D"] * 2),
                 ("wq_s", c["DT"] * c["D"] * 2),
                 ("wp_s", c["DT"] * c["D"] * 2),
                 ("kv_s", c["TPC"] * 2 * c["D"] * 2),
                 ("q_s", c["DT"] * c["CH"] * 2),
                 ("attB", 2 * c["D"] * 2),
                 ("kvaug", (c["NPAIR"] * 130 * 2 + 31) & ~31)]
        pos = 0
        bounds = []
        for nm, sz in suite:
            bounds.append((nm, pos, pos + sz))
            pos += sz
        return bounds

    def _win_kv(self, e):
        c = self.cfg
        ewsz = c["DT"] * c["O"] * 2
        lo, hi = e * ewsz, (e + 1) * ewsz
        for nm, rlo, rhi in self._suite_bounds():
            if nm == "kv_s" and rlo < hi and lo < rhi:
                return True
        return False

    def bind_we_span(self, e):
        c = self.cfg
        ewsz = c["DT"] * c["O"] * 2
        if e < self.n_in_c2:
            lo, hi = e * ewsz, (e + 1) * ewsz
            for nm, rlo, rhi in self._suite_bounds():
                if rlo < hi and lo < rhi:
                    self.merge_region(nm, f"we{e}")
        else:
            lo, hi = next((l, h) for nm, l, h in self.affn_spans
                          if nm == f"we{e}")
            for t in range(c["NT"]):
                tlo, thi = t * c["D"] * 4, (t + 1) * c["D"] * 4
                if tlo < hi and lo < thi:
                    self.merge_region(f"x{t}", f"we{e}")

    def bind_we_aliases(self):
        c = self.cfg
        for e in range(self.cfg["E"]):
            self.bind_we_span(e)
        for nm, lo, hi in self.affn_spans:
            if nm.startswith("we"):
                continue
            for t in range(c["NT"]):
                tlo, thi = t * c["D"] * 4, (t + 1) * c["D"] * 4
                if tlo < hi and lo < thi:
                    self.merge_region(f"x{t}", nm)

    def emit_ln_stats(self):
        for ch in range(self.cfg["NCH"]):
            self.emit_ln_stats_chunk(ch)
            self.emit_ln_chain(ch)

    def emit_ln_stats_chunk(self, ch):
        k = self
        c = self.cfg
        D, NT, NCH, TPC = c["D"], c["NT"], c["NCH"], c["TPC"]
        if True:
            tiles = [ch * TPC + m for m in range(TPC)]

            def f_v(eng, tiles=tiles, ch=ch):
                for t in tiles:
                    k.wait_w(eng, "vector", f"x{t}")
                    k.wait_rw(eng, "vector", f"s1c{ch}")
                    i = k.inc(eng.tensor_reduce(
                        k.st2.ap()[:, t * 2:t * 2 + 1],
                        k.xres.ap()[:, t * D:(t + 1) * D],
                        axis=AX.X, op=ALU.add), "s_v")
                    k.note_r(f"x{t}", "s_v", i)
                k.note_w(f"s1c{ch}", "s_v", i)
            self.on("vector", f_v)

            def f_s(eng, tiles=tiles, ch=ch):
                ns = max(1, (c["DT"] * c["CH"]) // D)
                for t in tiles:
                    m = t % ns
                    k.wait_w(eng, "scalar", f"x{t}")
                    k.wait_rw(eng, "scalar", "q_s")
                    k.wait_rw(eng, "scalar", f"s2c{ch}")
                    i = k.inc(eng.activation(
                        k.q_s.ap()[:, m * D:(m + 1) * D],
                        k.xres.ap()[:, t * D:(t + 1) * D],
                        AF.Square, accum_out=k.st2.ap()[:, t * 2 + 1:t * 2 + 2]),
                        "s_sc")
                    k.note_r(f"x{t}", "s_sc", i)
                    k.note_w("q_s", "s_sc", i)
                    k.note_w(f"q_sl{m}", "s_sc", i)
                k.note_w(f"s2c{ch}", "s_sc", i)
            self.on("scalar", f_s)

    def emit_ln_chain(self, ch):
        k = self
        c = self.cfg
        D, NT, NCH, TPC = c["D"], c["NT"], c["NCH"], c["TPC"]
        inv_d = 1.0 / D
        sl = slice(ch * TPC, (ch + 1) * TPC)
        ln = f"lnstat{ch}"

        def f_v2(eng):
            k.wait_w(eng, "vector", f"s1c{ch}")
            k.wait_w(eng, "vector", f"s2c{ch}")
            k.wait_rw(eng, "vector", ln)
            sv = k.st2.ap()[:, ch * TPC * 2:(ch + 1) * TPC * 2].rearrange(
                "p (t x) -> p t x", x=2)
            k.inc(eng.tensor_scalar(
                k.mean.ap()[:, sl], sv[:, :, 0:1].squeeze(2),
                scalar1=inv_d, scalar2=None, op0=ALU.mult), "s_v")
            k.wait(eng, "vector", "s_v", k.cnt("s_v"))
            k.inc(eng.tensor_tensor(
                k.mtmp.ap()[:, sl], k.mean.ap()[:, sl],
                k.mean.ap()[:, sl], op=ALU.mult), "s_v")
            k.wait(eng, "vector", "s_v", k.cnt("s_v"))
            k.inc(eng.scalar_tensor_tensor(
                k.varc.ap()[:, sl], sv[:, :, 1:2].squeeze(2), inv_d,
                k.mtmp.ap()[:, sl], op0=ALU.mult, op1=ALU.subtract), "s_v")
            k.wait(eng, "vector", "s_v", k.cnt("s_v"))
            i = k.inc(eng.tensor_scalar(
                k.varc.ap()[:, sl], k.varc.ap()[:, sl],
                scalar1=float(c["EPS"]), scalar2=None, op0=ALU.add), "s_v")
            k.note_w(ln, "s_v", i)
            k.note_r(f"s1c{ch}", "s_v", i)
            k.note_r(f"s2c{ch}", "s_v", i)
        self.on("vector", f_v2)

        def f_s2(eng):
            k.wait_w(eng, "scalar", ln)
            k.inc(eng.activation(
                k.rstd.ap()[:, sl], k.varc.ap()[:, sl], AF.Ln), "s_sc")
            k.wait(eng, "scalar", "s_sc", k.cnt("s_sc"))
            i = k.inc(eng.activation(
                k.rstd.ap()[:, sl], k.rstd.ap()[:, sl],
                AF.Exp, scale=-0.5), "s_sc")
            k.note_w(ln, "s_sc", i)
        self.on("scalar", f_s2)

        def f_v3(eng):
            k.wait_w(eng, "vector", ln)
            i = k.inc(eng.scalar_tensor_tensor(
                k.negm.ap()[:, sl], k.mean.ap()[:, sl], -1.0,
                k.rstd.ap()[:, sl], op0=ALU.mult, op1=ALU.mult), "s_v")
            k.note_w(ln, "s_v", i)
        self.on("vector", f_v3)

    def _ln_buf(self, dst):
        c = self.cfg
        if dst == "q":
            return (self.q_s, max(1, (c["DT"] * c["CH"]) // c["D"]),
                    "q_s", "q_sl")
        return (self.kv_s, c["TPC"] * 2, "kv_s", "kv_sl")

    def emit_ln_n(self, ch, dst="q"):
        k = self
        c = self.cfg
        D, TPC = c["D"], c["TPC"]
        buf, ns, coarse, slp = self._ln_buf(dst)
        for t in [ch * TPC + m for m in range(TPC)]:
            m = t % ns

            def f_n(eng, t=t, m=m):
                k.wait_w(eng, "vector", f"lnstat{t // self.cfg['TPC']}")
                k.wait_w(eng, "vector", f"x{t}")
                k.wait_rw(eng, "vector", coarse)
                j = k.inc(eng.scalar_tensor_tensor(
                    buf.ap()[:, m * D:(m + 1) * D],
                    k.xres.ap()[:, t * D:(t + 1) * D],
                    k.rstd.ap()[:, t:t + 1],
                    k.negm.ap()[:, t:t + 1].broadcast_to([128, D]),
                    op0=ALU.mult, op1=ALU.add), "s_v")
                k.note_r(f"x{t}", "s_v", j)
                k.note_w(coarse, "s_v", j)
                k.note_w(f"{slp}{m}", "s_v", j)
            self.on("vector", f_n)

    def emit_ln_t(self, ch, ev="vector", src="q", defer=False):
        k = self
        c = self.cfg
        D, DT, TPC = c["D"], c["DT"], c["TPC"]
        buf, ns, coarse, slp = self._ln_buf(src)
        sem = "s_v" if ev == "vector" else "s_sc"
        out = []
        for t in [ch * TPC + m for m in range(TPC)]:
            m = t % ns

            def f_pe(eng, t=t, m=m):
                bank = 6 + (t % 2)
                k.wait_w(eng, "tensor", "ident")
                k.wait_w(eng, "tensor", f"{slp}{m}")
                k.wait_rw(eng, "tensor", f"bank{bank}")
                for dt in range(DT):
                    inst = eng.transpose(
                        k.pbt[bank - 6].ap()[:, dt * 128:(dt + 1) * 128],
                        buf.ap()[:, m * D + dt * 128: m * D + (dt + 1) * 128],
                        k.ident.ap())
                i = k.inc(inst, "s_pe")
                k.note_w(f"bank{bank}", "s_pe", i)
                k.note_r(coarse, "s_pe", i)

            def f_v4(eng, t=t, ch=ch):
                bank = 6 + (t % 2)
                k.wait_w(eng, ev, f"bank{bank}")
                k.wait_rw(eng, ev, f"lnx{ch}")
                dst = k.lnx.ap().rearrange("p (dt s) -> p dt s", dt=DT)[
                    :, :, t * 128:(t + 1) * 128]
                srcv = k.pbt[bank - 6].ap()[:, 0:DT * 128].rearrange(
                    "p (dt n) -> p dt n", n=128)
                if ev == "vector":
                    i = k.inc(eng.tensor_copy(dst, srcv), sem)
                else:
                    i = k.inc(eng.copy(dst, srcv), sem)
                k.note_r(f"bank{bank}", sem, i)
                k.note_w(f"lnx{ch}", sem, i)
                k.note_w(f"lnxT{t}", sem, i)
            if defer:
                out.append([("tensor", f_pe), (ev, f_v4)])
            else:
                self.on("tensor", f_pe)
                self.on(ev, f_v4)
        return out

    def emit_batch(self, b):
        k = self
        c = self.cfg
        B, S, D, H, E, O = c["B"], c["S"], c["D"], c["H"], c["E"], c["O"]
        CH, FC, OC = c["CH"], c["FC"], c["OC"]
        DT, NT, NCH, TPC = c["DT"], c["NT"], c["NCH"], c["TPC"]
        NFC, NOC, NPAIR = c["NFC"], c["NOC"], c["NPAIR"]
        self._we_loaded = set()
        self._we_pend = []
        EH = E // 2
        nhalf = max(D // 512, 1)
        hw = min(512, D)
        NG = max(H // 4, 1)
        HPG = min(4, H)

        for dst in ["kvacc"]:
            self.merge_region("attA", dst)
            self.merge_region("scoresT", dst)
            self.merge_region("eb_s", dst)

        for ch in range(NCH):
            def f_x(eng, ch=ch):
                tiles = [ch * TPC + m for m in range(TPC)]
                for t in tiles:
                    k.wait_rw(eng, "sync", f"x{t}")
                k.wait_rw(eng, "sync", f"ffnA{ch}")
                for t in tiles:
                    self.deps_w[f"x{t}"] = {}
                    self.deps_r.pop(f"x{t}", None)
                row0 = b * S + ch * CH
                dst = k.xres.ap()[:, ch * TPC * D:(ch + 1) * TPC * D].rearrange(
                    "p (t d) -> p t d", t=TPC)
                src = k.dr["x"][row0:row0 + CH, :].rearrange(
                    "(t p) d -> p t d", p=128)
                i = k.inc(eng.dma_start(dst, src), f"s_x{ch}", 16)
                for t in tiles:
                    k.note_w(f"x{t}", f"s_x{ch}", i)
            self.on("sync", f_x)
            if b > 0 and ch < NFC:
                self.emit_wkv_reload(ch)
        if b > 0:
            for fc in range(min(NCH, NFC), NFC):
                self.emit_wkv_reload(fc)
        if b == 0:
            self._scr_total = 0
            self.emit_weight_loads(0)

        def cut_dump():
            def f_cd(eng):
                for t in range(NT):
                    k.wait_w(eng, "sync", f"x{t}")
                    row0 = b * S + t * 128
                    k.inc(eng.dma_start(
                        k.dr["out"][row0:row0 + 128, :],
                        k.xres.ap()[:, t * D:(t + 1) * D]), "s_st", 16)
                tot = k.cnt("s_st")
                for t in range(NT):
                    k.note_r(f"x{t}", "s_st", tot)
            self.on("sync", f_cd)

        cut = self.cfg.get("CUT")
        if cut == "load":
            cut_dump(); return

        self.emit_ln_stats_chunk(0)
        self.emit_ln_chain(0)
        if cut == "ln1":
            for ch in range(NCH):
                if ch > 0:
                    self.emit_ln_stats_chunk(ch)
                    self.emit_ln_chain(ch)
                self.emit_ln_n(ch)
                self.emit_ln_t(ch)
            cut_dump(); return

        for ch in range(NCH):
            if ch == 0:
                self.emit_ln_n(0)
            self.emit_ln_t(ch)
            if ch + 1 < NCH:
                self.emit_ln_stats_chunk(ch + 1)
                self.emit_ln_chain(ch + 1)
                self.emit_ln_n(ch + 1)
            for m in range(TPC):
                t = ch * TPC + m
                for fc in range(NFC):
                    bank = (m * NFC + fc) % 4
                    is_k = (fc * FC) < D

                    def f_kv(eng, t=t, m=m, fc=fc, bank=bank, ch=ch):
                        k.wait_w(eng, "tensor", f"lnxT{t}")
                        k.wait_w(eng, "tensor", f"wkv_fc{fc}")
                        k.wait_rw(eng, "tensor", f"bank{bank}")
                        for dt in range(DT):
                            inst = eng.matmul(
                                k.pb[bank].ap()[:, 0:FC],
                                k.lnx.ap()[:, dt * S + t * 128: dt * S + (t + 1) * 128],
                                k.wkv_s.ap()[:, dt * 2 * D + fc * FC:
                                             dt * 2 * D + (fc + 1) * FC],
                                start=(dt == 0), stop=(dt == DT - 1))
                        i = k.inc(inst, "s_pe")
                        k.note_w(f"bank{bank}", "s_pe", i)
                        k.note_r(f"lnx{ch}", "s_pe", i)
                        k.note_r("wkv_s", "s_pe", i)
                    self.on("tensor", f_kv)

                    if is_k:
                        def f_sq(eng, m=m, fc=fc, bank=bank):
                            kt = k.ktmp0 if bank % 2 == 0 else k.ktmp1
                            k.wait_w(eng, "scalar", f"bank{bank}")
                            k.wait_rw(eng, "scalar", f"ktmp{bank % 2}")
                            i = k.inc(eng.activation(
                                kt.ap()[:, 0:FC], k.pb[bank].ap()[:, 0:FC],
                                AF.Square), "s_sc")
                            k.note_w(f"ktmp{bank % 2}", "s_sc", i)
                            k.note_r(f"bank{bank}", "s_sc", i)
                        self.on("scalar", f_sq)

                        def f_ex(eng, m=m, fc=fc, bank=bank):
                            kt = k.ktmp0 if bank % 2 == 0 else k.ktmp1
                            k.wait_w(eng, "scalar", f"ktmp{bank % 2}")
                            k.wait_rw(eng, "scalar", "kv_s")
                            i = k.inc(eng.activation(
                                k.kv_s.ap()[:, m * 2 * D + fc * FC:
                                            m * 2 * D + (fc + 1) * FC],
                                kt.ap()[:, 0:FC], AF.Exp, scale=-0.5), "s_sc")
                            k.note_r(f"ktmp{bank % 2}", "s_sc", i)
                            k.note_w("kv_s", "s_sc", i)
                        self.on("scalar", f_ex)
                    else:
                        def f_cp(eng, m=m, fc=fc, bank=bank):
                            k.wait_w(eng, "scalar", f"bank{bank}")
                            k.wait_rw(eng, "scalar", "kv_s")
                            i = k.inc(eng.copy(
                                k.kv_s.ap()[:, m * 2 * D + fc * FC:
                                            m * 2 * D + (fc + 1) * FC],
                                k.pb[bank].ap()[:, 0:FC]), "s_sc")
                            k.note_r(f"bank{bank}", "s_sc", i)
                            k.note_w("kv_s", "s_sc", i)
                        self.on("scalar", f_cp)

            def f_kvaug(eng, ch=ch):
                k.wait_w(eng, "tensor", "kv_s")
                k.wait_w(eng, "tensor", "ones_col")
                for bk in ([4, 5] if NPAIR > 4 else [4]) + \
                        [2 + x for x in range(nhalf)]:
                    k.wait_rw(eng, "tensor", f"bank{bk}")
                for h in range(H):
                    p = h // 2
                    bank = 4 + (p // 4)
                    prow = (h % 2) * 64
                    pcol = (p % 4) * 65
                    for m in range(TPC):
                        inst = eng.matmul(
                            k.pb[bank].ap()[prow:prow + 64, pcol:pcol + 64],
                            k.kv_s.ap()[:, m * 2 * D + h * 64:
                                        m * 2 * D + h * 64 + 64],
                            k.kv_s.ap()[:, m * 2 * D + D + h * 64:
                                        m * 2 * D + D + h * 64 + 64],
                            start=(m == 0), stop=(m == TPC - 1))
                    for m in range(TPC):
                        inst = eng.matmul(
                            k.pb[bank].ap()[prow:prow + 64, pcol + 64:pcol + 65],
                            k.kv_s.ap()[:, m * 2 * D + h * 64:
                                        m * 2 * D + h * 64 + 64],
                            k.ones_col.ap(),
                            start=(m == 0), stop=(m == TPC - 1))
                i = k.inc(inst, "s_pe")
                for bk in ([4, 5] if NPAIR > 4 else [4]):
                    k.note_w(f"bank{bk}", "s_pe", i)
                k.note_r("kv_s", "s_pe", i)
                k.note_r("ones_col", "s_pe", i)
            self.on("tensor", f_kvaug)

            def f_kvacc(eng, ch=ch):
                op = ALU.add
                np_lo = min(NPAIR, 4)
                k.wait_w(eng, "vector", "bank4")
                k.wait_rw(eng, "vector", "kvacc")
                src = k.pb[4].ap()[:, 0:np_lo * 65]
                dstv = k.kvacc.ap()[:, 0:np_lo * 65]
                if ch == 0:
                    i = k.inc(eng.tensor_copy(dstv, src), "s_v")
                else:
                    i = k.inc(eng.tensor_tensor(dstv, dstv, src, op=op), "s_v")
                k.note_r("bank4", "s_v", i)
                if NPAIR > 4:
                    k.wait_w(eng, "vector", "bank5")
                    src = k.pb[5].ap()[:, 0:(NPAIR - 4) * 65]
                    dstv = k.kvacc.ap()[:, 4 * 65:NPAIR * 65]
                    if ch == 0:
                        i = k.inc(eng.tensor_copy(dstv, src), "s_v")
                    else:
                        i = k.inc(eng.tensor_tensor(dstv, dstv, src, op=op), "s_v")
                    k.note_r("bank5", "s_v", i)
                k.note_w("kvacc", "s_v", i)
            self.on("vector", f_kvacc)

        def f_kvev(eng):
            k.wait_rw(eng, "vector", "kvaug")
            k.wait_w(eng, "vector", "kvacc")
            kva = k.kvaug.ap().rearrange("p (pr x) -> p pr x", pr=NPAIR)
            k.inc(eng.memset(kva[0:64, :, 64:130], 0.0), "s_v")
            k.inc(eng.memset(kva[64:128, :, 0:64], 0.0), "s_v")
            k.inc(eng.memset(kva[64:128, :, 128:129], 0.0), "s_v")
            k.wait(eng, "vector", "s_v", k.cnt("s_v"))
            srcv = k.kvacc.ap().rearrange("p (pr x) -> p pr x", x=65)
            k.inc(eng.tensor_copy(
                kva[0:64, :, 0:64], srcv[0:64, :, 0:64]), "s_v")
            k.inc(eng.tensor_copy(
                kva[64:128, :, 64:128], srcv[64:128, :, 0:64]), "s_v")
            k.inc(eng.tensor_copy(
                kva[0:64, :, 128:129], srcv[0:64, :, 64:65]), "s_v")
            i = k.inc(eng.tensor_copy(
                kva[64:128, :, 129:130], srcv[64:128, :, 64:65]), "s_v")
            k.note_r("kvacc", "s_v", i)
            k.note_w("kvaug", "s_v", i)
        self.on("vector", f_kvev)

        for srcr in ["kvacc"]:
            self.merge_region(srcr, "attA")
        if cut == "p1":
            cut_dump(); return

        if b == 0:
            self.emit_scratch_w_qkv()
        ewsz = DT * O * 2
        for e in range(self.n_in_c2):
            if (e + 1) * ewsz <= DT * 2 * D * 2:
                self.emit_expert_load(e, b, stage="A")

        p2sub = self.cfg.get("P2SUB")
        self._p2_pend = []
        self._ln2_pend = []
        for ch in range(NCH):
            for f in range(DT):
                bank = f % 4

                def f_q(eng, f=f, bank=bank, ch=ch):
                    k.wait_w(eng, "tensor", f"lnx{ch}")
                    k.wait_w(eng, "tensor", "wq_s")
                    k.wait_rw(eng, "tensor", f"bank{bank}")
                    for dt in range(DT):
                        inst = eng.matmul(
                            k.pb[bank].ap()[:, 0:CH],
                            k.wq_s.ap()[:, dt * D + f * 128: dt * D + (f + 1) * 128],
                            k.lnx.ap()[:, dt * S + ch * CH: dt * S + (ch + 1) * CH],
                            start=(dt == 0), stop=(dt == DT - 1))
                    i = k.inc(inst, "s_pe")
                    k.note_w(f"bank{bank}", "s_pe", i)
                    k.note_r(f"lnx{ch}", "s_pe", i)
                    k.note_r("wq_s", "s_pe", i)
                self.on("tensor", f_q)

                def f_qe(eng, f=f, bank=bank):
                    kt = k.ktmp0 if bank % 2 == 0 else k.ktmp1
                    k.wait_w(eng, "scalar", f"bank{bank}")
                    k.wait_rw(eng, "scalar", f"ktmp{bank % 2}")
                    i = k.inc(eng.activation(
                        kt.ap()[:, 0:CH], k.pb[bank].ap()[:, 0:CH],
                        AF.Square), "s_sc")
                    k.note_w(f"ktmp{bank % 2}", "s_sc", i)
                    k.note_r(f"bank{bank}", "s_sc", i)
                self.on("scalar", f_qe)

                def f_qx(eng, f=f, bank=bank):
                    kt = k.ktmp0 if bank % 2 == 0 else k.ktmp1
                    k.wait_w(eng, "scalar", f"ktmp{bank % 2}")
                    k.wait_rw(eng, "scalar", "q_s")
                    i = k.inc(eng.activation(
                        k.q_s.ap()[:, f * CH:(f + 1) * CH],
                        kt.ap()[:, 0:CH], AF.Exp, scale=-0.5), "s_sc")
                    k.note_r(f"ktmp{bank % 2}", "s_sc", i)
                    k.note_w("q_s", "s_sc", i)
                self.on("scalar", f_qx)

            if p2sub == "q":
                continue
            for m in range(TPC):
                t = ch * TPC + m
                abank0 = 2 + (m % 2) * 2
                dbank = m % 2

                def f_att(eng, m=m, ch=ch, abank0=abank0, dbank=dbank):
                    k.wait_w(eng, "tensor", "q_s")
                    k.wait_w(eng, "tensor", "kvaug")
                    k.wait_rw(eng, "tensor", f"bank{abank0}")
                    k.wait_rw(eng, "tensor", f"bank{abank0 + 1}")
                    k.wait_rw(eng, "tensor", f"bank{dbank}")
                    for p in range(NPAIR):
                        bank = abank0 + p // 4
                        eng.matmul(
                            k.pb[bank].ap()[:, (p % 4) * 128:(p % 4 + 1) * 128],
                            k.q_s.ap()[:, p * CH + m * 128:
                                       p * CH + (m + 1) * 128],
                            k.kvaug.ap()[:, p * 130:p * 130 + 128],
                            start=True, stop=True)
                    for p in range(NPAIR):
                        inst = eng.matmul(
                            k.pb[dbank].ap()[:, p * 2:(p + 1) * 2],
                            k.q_s.ap()[:, p * CH + m * 128:
                                       p * CH + (m + 1) * 128],
                            k.kvaug.ap()[:, p * 130 + 128:p * 130 + 130],
                            start=True, stop=True)
                    i = k.inc(inst, "s_pe")
                    k.note_w(f"bank{abank0}", "s_pe", i)
                    k.note_w(f"bank{abank0 + 1}", "s_pe", i)
                    k.note_w(f"bank{dbank}", "s_pe", i)
                    k.note_r("q_s", "s_pe", i)
                    k.note_r("kvaug", "s_pe", i)
                self.on("tensor", f_att)

                while len(self._p2_pend) > 1:
                    for engn, fn in self._p2_pend.pop(0):
                        self.on(engn, fn)
                if self._ln2_pend:
                    for engn, fn in self._ln2_pend.pop(0):
                        self.on(engn, fn)

                def f_z(eng, m=m, ch=ch, abank0=abank0, dbank=dbank):
                    k.wait_w(eng, "vector", f"bank{dbank}")
                    k.wait_rw(eng, "vector", "r_all")
                    k.inc(eng.tensor_scalar(
                        k.r_all.ap(), k.pb[dbank].ap()[:, 0:H],
                        scalar1=float(c["ZEPS"]), scalar2=None,
                        op0=ALU.add), "s_v")
                    k.wait(eng, "vector", "s_v", k.cnt("s_v"))
                    i = k.inc(eng.reciprocal(k.r_all.ap(), k.r_all.ap()), "s_v")
                    k.note_w("r_all", "s_v", i)
                    k.note_r(f"bank{dbank}", "s_v", i)
                    k.wait(eng, "vector", "s_v", i)
                    k.wait_rw(eng, "vector", "attB")
                    for j in range(2 if NPAIR > 4 else 1):
                        bank = abank0 + j
                        npp = min(4, NPAIR - j * 4)
                        k.wait_w(eng, "vector", f"bank{bank}")
                        r = k.r_all.ap()[:, j * 8:j * 8 + npp * 2].unsqueeze(2)
                        i = k.inc(eng.tensor_tensor(
                            k.attB.ap()[:, (m % 2) * D + j * 512:
                                        (m % 2) * D + j * 512 + npp * 128
                                        ].rearrange("p (hh x) -> p hh x", x=64),
                            k.pb[bank].ap()[:, 0:npp * 128].rearrange(
                                "p (hh x) -> p hh x", x=64),
                            r.broadcast_to([128, npp * 2, 64]), op=ALU.mult),
                            "s_v")
                        k.note_r(f"bank{bank}", "s_v", i)
                    k.note_w("attB", "s_v", i)
                    k.note_r("r_all", "s_v", i)
                self.on("vector", f_z)

                ttpend = []

                def f_tt(eng, m=m, ch=ch):
                    bank = 6 + (m % 2)
                    k.wait_w(eng, "tensor", "attB")
                    k.wait_rw(eng, "tensor", f"bank{bank}")
                    for dt in range(DT):
                        inst = eng.transpose(
                            k.pbt[bank - 6].ap()[:, dt * 128:(dt + 1) * 128],
                            k.attB.ap()[:, (m % 2) * D + dt * 128:
                                        (m % 2) * D + (dt + 1) * 128],
                            k.ident.ap())
                    i = k.inc(inst, "s_pe")
                    k.note_w(f"bank{bank}", "s_pe", i)
                    k.note_r("attB", "s_pe", i)
                ttpend.append(("tensor", f_tt))

                def f_te(eng, m=m):
                    bank = 6 + (m % 2)
                    k.wait_w(eng, "scalar", f"bank{bank}")
                    k.wait_rw(eng, "scalar", "attA")
                    dst = k.attA.ap().rearrange("p (ct n) -> p ct n", ct=DT)[
                        :, :, m * 128:(m + 1) * 128]
                    src = k.pbt[bank - 6].ap()[:, 0:DT * 128].rearrange(
                        "p (ct n) -> p ct n", n=128)
                    i = k.inc(eng.copy(dst, src), "s_sc")
                    k.note_r(f"bank{bank}", "s_sc", i)
                    k.note_w("attA", "s_sc", i)
                    k.note_w(f"attA_m{m}", "s_sc", i)
                ttpend.append(("scalar", f_te))
                self._p2_pend.append(ttpend)
            if p2sub == "att" or p2sub == "tt":
                continue
            while self._p2_pend:
                for engn, fn in self._p2_pend.pop(0):
                    self.on(engn, fn)
            for m in range(TPC):
                t = ch * TPC + m
                for oc2 in range(nhalf):
                    bank = (m * nhalf + oc2) % 2

                    def f_pj(eng, m=m, oc2=oc2, bank=bank, t=t):
                        k.wait_w(eng, "tensor", f"attA_m{m}")
                        k.wait_w(eng, "tensor", "wp_s")
                        k.wait_w(eng, "tensor", "bproj_s")
                        k.wait_w(eng, "tensor", "ones_row")
                        k.wait_rw(eng, "tensor", f"bank{bank}")
                        for ct in range(DT):
                            eng.matmul(
                                k.pb[bank].ap()[:, 0:hw],
                                k.attA.ap()[:, ct * CH + m * 128:
                                            ct * CH + (m + 1) * 128],
                                k.wp_s.ap()[:, ct * D + oc2 * hw:
                                            ct * D + (oc2 + 1) * hw],
                                start=(ct == 0), stop=False)
                        inst = eng.matmul(
                            k.pb[bank].ap()[:, 0:hw],
                            k.ones_row.ap(),
                            k.bproj_s.ap()[0:1, oc2 * hw:(oc2 + 1) * hw],
                            start=False, stop=True)
                        i = k.inc(inst, "s_pe")
                        k.note_w(f"bank{bank}", "s_pe", i)
                        k.note_r("attA", "s_pe", i)
                        k.note_r("wp_s", "s_pe", i)
                        k.note_r("bproj_s", "s_pe", i)
                    self.on("tensor", f_pj)

                    def f_pe2(eng, m=m, oc2=oc2, bank=bank, t=t):
                        k.wait_w(eng, "vector", f"bank{bank}")
                        k.wait_rw(eng, "vector", f"x{t}")
                        xsl = k.xres.ap()[:, t * D + oc2 * hw:
                                          t * D + (oc2 + 1) * hw]
                        i = k.inc(eng.tensor_tensor(
                            xsl, xsl, k.pb[bank].ap()[:, 0:hw], op=ALU.add), "s_v")
                        k.note_r(f"bank{bank}", "s_v", i)
                        k.note_w(f"x{t}", "s_v", i)
                    self.on("vector", f_pe2)

            self.emit_ln_stats_chunk(ch)
            self.emit_ln_chain(ch)
            self.emit_ln_n(ch, dst="kv")
            self._ln2_pend.extend(
                self.emit_ln_t(ch, ev="scalar", src="kv", defer=True))

        if cut == "p2":
            while self._ln2_pend:
                for engn, fn in self._ln2_pend.pop(0):
                    self.on(engn, fn)
            cut_dump(); return

        sb = {nm: (rlo, rhi) for nm, rlo, rhi in self._suite_bounds()}

        def _win_overlaps(e, names):
            lo, hi = e * ewsz, (e + 1) * ewsz
            return any(sb[nm][0] < hi and lo < sb[nm][1] for nm in names)

        for e in range(self.n_in_c2):
            if e not in self._we_loaded and \
                    not _win_overlaps(e, ["kv_s", "q_s", "attB", "kvaug"]):
                self.emit_expert_load(e, b, stage="B")

        self._x2_dumped = {}

        def f_dump(eng):
            for t in range(NT):
                k.wait_w(eng, "sync", f"x{t}")
                row0 = b * S + t * 128
                k.inc(eng.dma_start(
                    k.dr["out"][row0:row0 + 128, :],
                    k.xres.ap()[:, t * D:(t + 1) * D]), "s_st", 16)
            tot = k.cnt("s_st")
            for t in range(NT):
                k.note_r(f"x{t}", "s_st", tot)
                self._x2_dumped[t] = tot
        self.on("sync", f_dump)

        for e in range(self.n_in_c2):
            if e not in self._we_loaded and not _win_overlaps(e, ["kv_s"]):
                self.emit_expert_load(e, b, stage="C")
        for e in range(self.n_in_c2, E):
            self.emit_expert_load(e, b, stage="C")

        self.bind_we_aliases()
        self.merge_region("attA", "scoresT")
        self.merge_region("attA", "eb_s")

        def f_eb(eng):
            k.wait_rw(eng, "gpsimd", "eb_s")
            self.deps_w["eb_s"] = {}
            self.deps_r.pop("eb_s", None)
            k.inc(eng.dma_start(k.eb_s.ap(), k.dr["eb"]), "s_eb", 16)
            k.wait_rw(eng, "gpsimd", "sel8")
            self.deps_w["sel8"] = {}
            self.deps_r.pop("sel8", None)
            i = k.inc(eng.dma_start(k.sel8.ap(), k.dr["sel8"]), "s_eb", 16)
            k.note_w("eb_s", "s_eb", i)
            k.note_w("sel8", "s_eb", i)
        self.on("gpsimd", f_eb)



        self.merge_region("ktmp0", "gate")
        self.merge_region("ktmp1", "gate")
        for t in range(NT):
            ch = t // TPC
            bank = t % 2

            def f_g(eng, t=t, ch=ch, bank=bank):
                k.wait_w(eng, "tensor", f"lnxT{t}")
                k.wait_w(eng, "tensor", "gw_s")
                k.wait_w(eng, "tensor", "gb_s")
                k.wait_w(eng, "tensor", "ones_row")
                k.wait_rw(eng, "tensor", f"bank{bank}")
                for dt in range(DT):
                    eng.matmul(
                        k.pb[bank].ap()[:, 0:E],
                        k.lnx.ap()[:, dt * S + t * 128: dt * S + (t + 1) * 128],
                        k.gw_s.ap()[:, dt * E:(dt + 1) * E],
                        start=(dt == 0), stop=False)
                inst = eng.matmul(
                    k.pb[bank].ap()[:, 0:E], k.ones_row.ap(),
                    k.gb_s.ap()[0:1, 0:E], start=False, stop=True)
                i = k.inc(inst, "s_pe")
                k.note_w(f"bank{bank}", "s_pe", i)
                k.note_r(f"lnx{ch}", "s_pe", i)
                k.note_r("gw_s", "s_pe", i)
                k.note_r("gb_s", "s_pe", i)
            self.on("tensor", f_g)

            def f_ge(eng, t=t, bank=bank):
                k.wait_w(eng, "vector", f"bank{bank}")
                k.wait_rw(eng, "vector", "gate")
                i = k.inc(eng.tensor_copy(
                    k.lgraw.ap()[:, t * E:(t + 1) * E],
                    k.pb[bank].ap()[:, 0:E]), "s_v")
                k.note_r(f"bank{bank}", "s_v", i)
                k.note_w("gate", "s_v", i)
            self.on("vector", f_ge)
            if self._ln2_pend and t % 3 == 2:
                for engn, fn in self._ln2_pend.pop(0):
                    self.on(engn, fn)

        while self._ln2_pend:
            for engn, fn in self._ln2_pend.pop(0):
                self.on(engn, fn)

        for e in range(E):
            if e not in self._we_loaded:
                self.emit_expert_load(e, b, stage="D")
        self.flush_we_notes()

        def v3(ap):
            return ap.rearrange("p (t e) -> p t e", e=E)

        def f_t2(eng):
            k.wait_w(eng, "vector", "gate")
            k.inc(eng.tensor_reduce(
                k.m1.ap(), v3(k.lgraw.ap()), axis=AX.X, op=ALU.max), "s_v")
            k.wait(eng, "vector", "s_v", k.cnt("s_v"))
            k.inc(eng.tensor_tensor(
                v3(k.gt1.ap()), v3(k.lgraw.ap()),
                k.m1.ap().unsqueeze(2).broadcast_to([128, NT, E]),
                op=ALU.is_equal), "s_v")
            k.wait(eng, "vector", "s_v", k.cnt("s_v"))
            k.inc(eng.scalar_tensor_tensor(
                k.gt2.ap(), k.gt1.ap(), -1e30, k.lgraw.ap(),
                op0=ALU.mult, op1=ALU.add), "s_v")
            k.wait(eng, "vector", "s_v", k.cnt("s_v"))
            k.inc(eng.tensor_reduce(
                k.m2.ap(), v3(k.gt2.ap()), axis=AX.X, op=ALU.max), "s_v")
            k.wait(eng, "vector", "s_v", k.cnt("s_v"))
            i = k.inc(eng.tensor_tensor(
                k.ed.ap(), k.m2.ap(), k.m1.ap(), op=ALU.subtract), "s_v")
            k.note_w("gate", "s_v", i)
        self.on("vector", f_t2)

        def f_t2s(eng):
            k.wait_w(eng, "scalar", "gate")
            i = k.inc(eng.activation(k.ed.ap(), k.ed.ap(), AF.Exp), "s_sc")
            k.note_w("gate", "s_sc", i)
        self.on("scalar", f_t2s)

        def f_t2b(eng):
            k.wait_w(eng, "vector", "gate")
            k.inc(eng.tensor_scalar(
                k.rden.ap(), k.ed.ap(), scalar1=1.0, scalar2=None,
                op0=ALU.add), "s_v")
            k.wait(eng, "vector", "s_v", k.cnt("s_v"))
            k.inc(eng.reciprocal(k.rden.ap(), k.rden.ap()), "s_v")
            k.inc(eng.tensor_tensor(
                v3(k.gt1.ap()), v3(k.lgraw.ap()),
                k.m2.ap().unsqueeze(2).broadcast_to([128, NT, E]),
                op=ALU.is_ge), "s_v")
            i = k.inc(eng.tensor_tensor(
                v3(k.gt2.ap()), v3(k.lgraw.ap()),
                k.m1.ap().unsqueeze(2).broadcast_to([128, NT, E]),
                op=ALU.subtract), "s_v")
            k.note_w("gate", "s_v", i)
        self.on("vector", f_t2b)

        def f_t2e(eng):
            k.wait_w(eng, "scalar", "gate")
            i = k.inc(eng.activation(k.gt2.ap(), k.gt2.ap(), AF.Exp), "s_sc")
            k.note_w("gate", "s_sc", i)
        self.on("scalar", f_t2e)

        def f_t2f(eng):
            k.wait_w(eng, "vector", "gate")
            k.inc(eng.tensor_tensor(
                k.gt2.ap(), k.gt2.ap(), k.gt1.ap(), op=ALU.mult), "s_v")
            k.wait(eng, "vector", "s_v", k.cnt("s_v"))
            i = k.inc(eng.tensor_tensor(
                v3(k.scores.ap()), v3(k.gt2.ap()),
                k.rden.ap().unsqueeze(2).broadcast_to([128, NT, E]),
                op=ALU.mult), "s_v")
            k.note_w("gate", "s_v", i)
            k.note_w("scores_done", "s_v", i)
        self.on("vector", f_t2f)

        def f_sbf(eng):
            k.wait_w(eng, "vector", "gate")
            i = k.inc(eng.tensor_copy(
                k.scores_bf.ap(), k.scores.ap()), "s_v")
            k.note_w("gate", "s_v", i)
        self.on("vector", f_sbf)
        for t4 in range(0, NT, 4):
            def f_st(eng, t4=t4):
                k.wait_w(eng, "tensor", "gate")
                k.wait_w(eng, "tensor", "ident")
                k.wait_rw(eng, "tensor", "bank6")
                for j in range(min(4, NT - t4)):
                    t = t4 + j
                    inst = eng.transpose(
                        k.pbt[0].ap()[0:E, j * 128:(j + 1) * 128],
                        k.scores_bf.ap()[:, t * E:(t + 1) * E],
                        k.ident.ap())
                i = k.inc(inst, "s_pe")
                k.note_w("bank6", "s_pe", i)
                k.note_r("gate", "s_pe", i)
            self.on("tensor", f_st)

            def f_ste(eng, t4=t4):
                nj = min(4, NT - t4)
                k.wait_w(eng, "vector", "bank6")
                k.wait_rw(eng, "vector", "scoresT")
                i = k.inc(eng.tensor_copy(
                    k.scoresT.ap()[0:E, t4 * 128:t4 * 128 + nj * 128],
                    k.pbt[0].ap()[0:E, 0:nj * 128]), "s_v")
                k.note_r("bank6", "s_v", i)
                k.note_w("scoresT", "s_v", i)
                k.note_w(f"scoresT_g{t4 // 4}", "s_v", i)
            self.on("vector", f_ste)

        if cut == "gate":
            return

        def emit_z8(t):
            def f_z8(eng, t=t):
                k.wait_w(eng, "tensor", f"scoresT_g{t // 4}")
                k.wait_w(eng, "tensor", "sel8")
                for e in range(E):
                    bank = 4 + e // 4
                    if e % 4 == 0:
                        k.wait_rw(eng, "tensor", f"bank{bank}")
                    inst = eng.matmul(
                        k.pb[bank].ap()[:, (e % 4) * 128:(e % 4 + 1) * 128],
                        k.sel8.ap()[0:E, e * 128:(e + 1) * 128],
                        k.scoresT.ap()[0:E, t * 128:(t + 1) * 128],
                        start=True, stop=True)
                    if e % 4 == 3 or e == E - 1:
                        i = k.inc(inst, "s_pe")
                        k.note_w(f"bank{bank}", "s_pe", i)
                k.note_r("scoresT", "s_pe", k.cnt("s_pe"))
            self.on("tensor", f_z8)

        emit_z8(0)
        self._pend_evac = []
        for t in range(NT):
            ch = t // TPC

            def f_z8e(eng, t=t):
                k.wait_rw(eng, "vector", "z8")
                nb = max(1, E // 4)
                for j in range(nb):
                    bank = 4 + j
                    k.wait_w(eng, "vector", f"bank{bank}")
                    nn = min(4, E - j * 4) * 128
                    i = k.inc(eng.tensor_copy(
                        k.z8.ap()[:, j * 512:j * 512 + nn],
                        k.pb[bank].ap()[:, 0:nn]), "s_v")
                    k.note_r(f"bank{bank}", "s_v", i)
                k.note_w("z8", "s_v", i)
            self.on("vector", f_z8e)

            for half in (0, 1):
                lhname = "lhsT_lo" if half == 0 else "lhsT_hi"

                def f_mk(eng, t=t, ch=ch, half=half, lhname=lhname):
                    dst = getattr(k, lhname)
                    k.wait_w(eng, "vector", "z8")
                    k.wait_w(eng, "vector", f"lnxT{t}")
                    k.wait_rw(eng, "vector", lhname)
                    in0 = k.lnx.ap().rearrange("p (dt s) -> p dt s", dt=DT)[
                        :, :, t * 128:(t + 1) * 128].unsqueeze(1).broadcast_to(
                        [128, EH, DT, 128])
                    in1 = k.z8.ap()[:, half * EH * 128:(half + 1) * EH * 128
                                    ].rearrange("p (e n) -> p e n", e=EH
                                                ).unsqueeze(2).broadcast_to(
                        [128, EH, DT, 128])
                    i = k.inc(eng.tensor_tensor(
                        dst.ap().rearrange("p (e dt n) -> p e dt n",
                                           e=EH, dt=DT),
                        in0, in1, op=ALU.mult), "s_v")
                    k.note_w(lhname, "s_v", i)
                    k.note_r(f"lnx{ch}", "s_v", i)
                    k.note_r("z8", "s_v", i)
                self.on("vector", f_mk)

            if t + 1 < NT:
                emit_z8(t + 1)

            banks = [(t % 2) * 2 + oc for oc in range(NOC)]

            def f_em(eng, t=t, banks=banks):
                k.wait_w(eng, "tensor", f"scoresT_g{t // 4}")
                k.wait_w(eng, "tensor", "eb_s")
                for oc in range(NOC):
                    bank = banks[oc]
                    k.wait_rw(eng, "tensor", f"bank{bank}")
                    eng.matmul(
                        k.pb[bank].ap()[:, 0:OC],
                        k.scoresT.ap()[0:E, t * 128:(t + 1) * 128],
                        k.eb_s.ap()[0:E, oc * OC:(oc + 1) * OC],
                        start=True, stop=False)
                stA = [e for e in range(self.n_in_c2)
                       if (e + 1) * DT * O * 2 <= DT * 2 * D * 2]
                stRA = list(range(self.n_in_c2, E))
                late = [e for e in range(self.n_in_c2)
                        if e not in stA and self._win_kv(e)]
                rest = [e for e in range(E)
                        if e not in stA + stRA + late]
                eorder = stA + stRA + rest + late
                half_left = {0: sum(1 for e in eorder if e < EH),
                             1: sum(1 for e in eorder if e >= EH)}
                for idx, e in enumerate(eorder):
                    half = e // EH
                    lhname = "lhsT_lo" if half == 0 else "lhsT_hi"
                    lh = getattr(k, lhname)
                    k.wait_w(eng, "tensor", lhname)
                    k.wait_w(eng, "tensor", f"we{e}")
                    last_inst = None
                    for oc in range(NOC):
                        bank = banks[oc]
                        for dt in range(DT):
                            last_inst = eng.matmul(
                                k.pb[bank].ap()[:, 0:OC],
                                lh.ap()[:, ((e % EH) * DT + dt) * 128:
                                        ((e % EH) * DT + dt + 1) * 128],
                                k.we[e].ap()[:, dt * O + oc * OC:
                                             dt * O + (oc + 1) * OC],
                                start=False,
                                stop=(idx == E - 1 and dt == DT - 1))
                    i = k.inc(last_inst, "s_pe")
                    k.note_r(f"we{e}", "s_pe", i)
                    half_left[half] -= 1
                    if half_left[half] == 0:
                        k.note_r(lhname, "s_pe", i)
                    if idx == E - 1:
                        for bank in banks:
                            k.note_w(f"bank{bank}", "s_pe", i)
                        k.note_r("scoresT", "s_pe", i)
                        k.note_r("eb_s", "s_pe", i)
            self.on("tensor", f_em)

            pend = []
            for oc in range(NOC):
                bank = banks[oc]
                pidx = (t * NOC + oc) % 2
                ot = self.otmp0 if pidx == 0 else self.otmp1
                otname = f"otmp{pidx}"

                def f_oe(eng, bank=bank, ot=ot, otname=otname):
                    k.wait_w(eng, "vector", f"bank{bank}")
                    k.wait_rw(eng, "vector", otname)
                    i = k.inc(eng.tensor_copy(
                        ot.ap(), k.pb[bank].ap()[:, 0:OC]), "s_v")
                    k.note_r(f"bank{bank}", "s_v", i)
                    k.note_w(otname, "s_v", i)
                pend.append(("vector", f_oe))

                def f_oa(eng, t=t, oc=oc, ot=ot, otname=otname, pidx=pidx):
                    k.wait_w(eng, "gpsimd", otname)
                    k.wait(eng, "gpsimd", "s_st", self._x2_dumped[t])
                    row0 = b * S + t * 128
                    i = k.inc(eng.dma_start(
                        k.dr["out"][row0:row0 + 128, oc * OC:(oc + 1) * OC],
                        ot.ap(), accum_op=ALU.add), f"s_ac{pidx}", 16)
                    k.note_r(otname, f"s_ac{pidx}", i)
                pend.append(("gpsimd", f_oa))
            self._pend_evac.append(pend)
            while len(self._pend_evac) > 1:
                for engn, fn in self._pend_evac.pop(0):
                    self.on(engn, fn)

        while self._pend_evac:
            for engn, fn in self._pend_evac.pop(0):
                self.on(engn, fn)
        if b == 0:
            self.emit_scratch_we()
        spanmap = {nm: (lo, hi) for nm, lo, hi in self.affn_spans}
        chb = TPC * D * 4
        for r in ["lhsT_lo", "lhsT_hi", "z8", "otmp0", "otmp1", "sel8"] + \
                 [f"we{e}" for e in range(self.n_in_c2, E)]:
            lo, hi = spanmap[r]
            for ch2 in range(lo // chb, min(NCH, (hi + chb - 1) // chb)):
                self.merge_region(r, f"ffnA{ch2}")
        self.merge_region("gate", "ktmp0")
        self.merge_region("gate", "ktmp1")
        ewsz = DT * O * 2
        suite = [("wkv_s", DT * 2 * D * 2), ("wq_s", DT * D * 2),
                 ("wp_s", DT * D * 2), ("kv_s", TPC * 2 * D * 2),
                 ("q_s", DT * CH * 2), ("attB", 2 * D * 2),
                 ("kvaug", (NPAIR * 130 * 2 + 31) & ~31)]
        pos = 0
        sbounds = []
        for nm, sz in suite:
            sbounds.append((nm, pos, pos + sz))
            pos += sz
        for e in range(self.n_in_c2):
            lo, hi = e * ewsz, (e + 1) * ewsz
            for nm, rlo, rhi in sbounds:
                if rlo < hi and lo < rhi:
                    self.merge_region(f"we{e}", nm)
        self.merge_region("scoresT", "attA")
        self.merge_region("eb_s", "attA")
        if b + 1 < B:
            self.emit_weight_loads(b + 1)

    def emit_dbg(self):
        k = self
        c = self.cfg

        def f(eng):
            col = [0]

            def dump(ap, n):
                k.inc(eng.dma_start(
                    k.dr["dbg"][0:ap.shape[0], col[0]:col[0] + n], ap),
                    "s_st", 16)
                col[0] += n
            for s in ["s_v", "s_sc", "s_pe"]:
                k.wait(eng, "gpsimd", s, k.cnt(s))
            dump(k.scores.ap(), c["NT"] * c["E"])
            dump(k.scoresT.ap(), c["S"])
            dump(k.z8.ap(), c["E"] * 128)
            dump(k.lhsT_lo.ap(), (c["E"] // 2) * c["DT"] * 128)
            dump(k.lhsT_hi.ap(), (c["E"] // 2) * c["DT"] * 128)
            dump(k.lnx.ap()[:, 0:2048], 2048)
            dump(k.kvaug.ap(), c["NPAIR"] * 130)
        self.on("gpsimd", f)

    def emit_tail(self):
        if self.cfg.get("DBG"):
            self.emit_dbg()
        k = self

        def f(eng):
            for s in (["s_ac0", "s_ac1", "s_wl", "s_st", "s_eb"]
                      + [f"s_we{st}" for st in "ABCD"]
                      + [f"s_ws{st}" for st in "ABCD"]):
                if k.cnt(s):
                    eng.wait_ge(k.sems[s], k.cnt(s))
        self.on("gpsimd", f)

        def f2(eng):
            if k.cnt("s_st"):
                eng.wait_ge(k.sems["s_st"], k.cnt("s_st"))
        self.on("sync", f2)


def prep_inputs(cfg, x, W_qkv, W_proj, b_proj, gate_w, gate_b, expert_w,
                expert_b):
    import ml_dtypes
    D = cfg["D"]
    m = {}
    m["x"] = np.ascontiguousarray(x.reshape(-1, D).astype(np.float32))
    m["wkvT"] = np.ascontiguousarray(np.asarray(W_qkv)[D:3 * D, :].T)
    m["wqT"] = np.ascontiguousarray(np.asarray(W_qkv)[0:D, :].T)
    m["wpT"] = np.ascontiguousarray(np.asarray(W_proj).T)
    m["bproj"] = np.ascontiguousarray(np.asarray(b_proj).reshape(1, D))
    m["gwT"] = np.ascontiguousarray(np.asarray(gate_w).T)
    m["gb"] = np.ascontiguousarray(np.asarray(gate_b).reshape(1, -1))
    m["ew"] = np.ascontiguousarray(np.asarray(expert_w))
    m["eb"] = np.ascontiguousarray(np.asarray(expert_b))
    m["ident"] = np.eye(128, dtype=ml_dtypes.bfloat16)
    m["ones_row"] = np.ones((1, 128), ml_dtypes.bfloat16)
    E = m["eb"].shape[0]
    sel = np.zeros((E, E * 128), np.float32)
    for e in range(E):
        sel[e, e * 128:(e + 1) * 128] = 1.0
    m["sel8"] = sel.astype(ml_dtypes.bfloat16)
    m["ones_col"] = np.ones((128, 1), ml_dtypes.bfloat16)
    return m






from concourse.bass_utils import run_bass_kernel_spmd

_NC_CACHE = {}




from concourse.bass_utils import run_bass_kernel_spmd

_NC_CACHE = {}


def _get_program():
    if "nc" not in _NC_CACHE:
        bl = Builder(default_cfg())
        _NC_CACHE["nc"] = bl.build()
    return _NC_CACHE["nc"]


def kernel(L0_attn_x, L0_attn_W_qkv, L0_attn_W_proj, L0_attn_b_proj,
           L0_ffn_gate_w, L0_ffn_gate_b, L0_ffn_expert_w, L0_ffn_expert_b):
    cfg = default_cfg()
    n_cores = 8
    B_total = 32
    Bc = B_total // n_cores
    x = np.ascontiguousarray(np.asarray(L0_attn_x, dtype=np.float32))
    nc = _get_program()
    base = prep_inputs(cfg, x[0:Bc],
                       np.asarray(L0_attn_W_qkv, np.float32),
                       np.asarray(L0_attn_W_proj, np.float32),
                       np.asarray(L0_attn_b_proj, np.float32),
                       np.asarray(L0_ffn_gate_w, np.float32),
                       np.asarray(L0_ffn_gate_b, np.float32),
                       np.asarray(L0_ffn_expert_w, np.float32),
                       np.asarray(L0_ffn_expert_b, np.float32))
    in_maps = []
    for c in range(n_cores):
        m = dict(base)
        m["x"] = np.ascontiguousarray(
            x[c * Bc:(c + 1) * Bc].reshape(Bc * cfg["S"], cfg["D"]))
        in_maps.append(m)
    res = run_bass_kernel_spmd(nc, in_maps, core_ids=list(range(n_cores)))
    out = np.concatenate(
        [np.asarray(r["out"]).reshape(Bc, cfg["S"], cfg["D"])
         for r in res.results], axis=0)
    return out.astype(np.float32)
